# version 13
# baseline (speedup 1.0000x reference)
import contextlib
import numpy as np
import ml_dtypes
import concourse.bass as bass
import concourse.mybir as mybir
from concourse.bass_utils import run_bass_kernel_spmd

F32 = mybir.dt.float32
BF16 = mybir.dt.bfloat16
AF = mybir.ActivationFunctionType
ALU = mybir.AluOpType
AX = mybir.AxisListType

NCORE = 8
T = 2048
D = 1024
NT = 16
NKC = 8
NQC = 4
NPAIR = 8
E_DYN = 8
CAP = 5120.0
BIG = 1.0e30
EPS = 1e-5

UNITS = [("d", e) for e in range(E_DYN)] + [("s", q) for q in range(4)]
NU = len(UNITS)


TAP = None


def build_nc(tap=None):
    nc = bass.Bass("TRN2", target_bir_lowering=False, debug=False)
    dbg2 = None
    if tap is not None:
        dbg2 = nc.dram_tensor("dbg", [128, 32768], F32,
                              kind="ExternalOutput").ap()

    xc = nc.dram_tensor("xc", [2 * T, D], F32, kind="ExternalInput").ap()
    wq = nc.dram_tensor("wq", [D, D], F32, kind="ExternalInput").ap()
    wk = nc.dram_tensor("wk", [D, D], F32, kind="ExternalInput").ap()
    wv = nc.dram_tensor("wv", [D, D], F32, kind="ExternalInput").ap()
    gwt = nc.dram_tensor("gwt", [D, 9], F32, kind="ExternalInput").ap()
    wg_t = nc.dram_tensor("wg_t", [E_DYN, D, 512], F32, kind="ExternalInput").ap()
    wu_t = nc.dram_tensor("wu_t", [E_DYN, D, 512], F32, kind="ExternalInput").ap()
    wd_t = nc.dram_tensor("wd_t", [E_DYN, 512, D], F32, kind="ExternalInput").ap()
    wsg_t = nc.dram_tensor("wsg_t", [D, 2048], F32, kind="ExternalInput").ap()
    wsu_t = nc.dram_tensor("wsu_t", [D, 2048], F32, kind="ExternalInput").ap()
    wsd_t = nc.dram_tensor("wsd_t", [2048, D], F32, kind="ExternalInput").ap()
    meta = nc.dram_tensor("meta", [8, 2], F32, kind="ExternalInput").ap()
    outc = nc.dram_tensor("outc", [2 * T, D], F32, kind="ExternalOutput").ap()

    gu_cache = nc.dram_tensor("gu_cache", [NU, 2, 128, NKC, 512], BF16,
                              kind="Internal")
    wd_cache = nc.dram_tensor("wd_cache", [NU, 128, 4, D], BF16,
                              kind="Internal")
    ag_in = [nc.dram_tensor(f"ag_in{s}", [1, 8], F32, kind="Internal")
             for s in (0, 1)]
    ag_out = [nc.dram_tensor(f"ag_out{s}", [8, 8], F32, kind="Internal",
                             addr_space="Shared") for s in (0, 1)]

    ident_d = nc.inline_tensor(np.eye(128, dtype=ml_dtypes.bfloat16),
                               name="ident").ap()
    tri_d = nc.inline_tensor(
        np.triu(np.ones((128, 128))).astype(ml_dtypes.bfloat16), name="tri").ap()
    ones_d = nc.inline_tensor(
        np.ones((128, 128), dtype=ml_dtypes.bfloat16), name="ones").ap()
    onesf_d = nc.inline_tensor(
        np.ones((1, 128), dtype=np.float32), name="onesf").ap()
    sel_np = np.zeros((9, 9 * 128), dtype=ml_dtypes.bfloat16)
    for _e in range(9):
        sel_np[_e, _e * 128:(_e + 1) * 128] = 1
    sel_d = nc.inline_tensor(sel_np, name="sel").ap()

    xc_t = xc.rearrange("(s t p) d -> s t p d", s=2, p=128)
    wq_v = wq.rearrange("(kc p) o -> p kc o", p=128)
    wk_v = wk.rearrange("(kc p) o -> p kc o", p=128)
    wv_v = wv.rearrange("(kc p) o -> p kc o", p=128)
    gwt_v = gwt.rearrange("(kc p) o -> p kc o", p=128)
    wg_v = wg_t.rearrange("e (kc p) i -> e p kc i", p=128)
    wu_v = wu_t.rearrange("e (kc p) i -> e p kc i", p=128)
    wd_v = wd_t.rearrange("e (ic p) h -> e p ic h", p=128)
    wsg_v = wsg_t.rearrange("(kc p) i -> p kc i", p=128)
    wsu_v = wsu_t.rearrange("(kc p) i -> p kc i", p=128)
    wsd_v = wsd_t.rearrange("(ic p) h -> p ic h", p=128)
    out_t = outc.rearrange("(s t p) d -> s t p d", s=2, p=128)

    ops = {k: [] for k in ("tensor", "vector", "scalar", "sync", "gpsimd")}
    cnt = {}
    sem_names = ["xld", "wld", "t", "v", "sc", "g", "st"]
    for s in sem_names:
        cnt[s] = 0
    sems = {}

    ENG_SEM = {"tensor": "t", "vector": "v", "scalar": "sc", "gpsimd": "g",
               "sync": "st"}

    def op(eng, fn, wait=(), inc=(), dep=False):
        w = list(wait)
        if dep and ops[eng]:
            sem = ENG_SEM[eng]
            prev = ops[eng][-1]
            if not any(s == sem for s, _ in prev["inc"]):
                prev["inc"].append((sem, 1))
                cnt[sem] += 1
            w.append((sem, cnt[sem]))
        rec = {"fn": fn, "wait": w, "inc": list(inc)}
        for s, n in inc:
            cnt[s] += n
        ops[eng].append(rec)

    def snap(s):
        return cnt[s]

    bankw = {}

    def bwait(*banks):
        return [bankw[b] for b in banks if b in bankw]

    def brel(b, sem):
        bankw[b] = (sem, cnt[sem])

    def tap_dump(name, waits, items, sq):
        if tap != name or sq != 0:
            return
        off = 0
        for ap_in, n, cast in items:
            eng = "gpsimd" if cast else "sync"
            op(eng, lambda e, ap_in=ap_in, off=off, n=n: e.dma_start(
                out=dbg2[0:ap_in.shape[0], off:off + n], in_=ap_in),
                wait=waits, inc=[("st", 16)])
            off += n

    with contextlib.ExitStack() as es:
        E = es.enter_context
        for s in sem_names:
            sems[s] = E(nc.semaphore(s))

        ps4 = E(nc.psum_tensor("ps4", [128, 2048], F32))
        pe = E(nc.psum_tensor("pe", [128, 512], F32))
        pm = E(nc.psum_tensor("pm", [128, 512], F32))
        pt = E(nc.psum_tensor("pt", [128, 2048], BF16))

        def ps4q(i):
            return ps4[:, i * 512:(i + 1) * 512]

        def pth(i):
            return pt[:, i * 1024:(i + 1) * 1024]

        x_sb = E(nc.sbuf_tensor("x_sb", [128, NT, D], F32))
        xf = E(nc.sbuf_tensor("xf", [128, NKC, T], BF16))
        c_id = E(nc.sbuf_tensor("c_id", [128, 128], BF16))
        c_tri = E(nc.sbuf_tensor("c_tri", [128, 128], BF16))
        c_ones = E(nc.sbuf_tensor("c_ones", [128, 128], BF16))
        gwt_sb = E(nc.sbuf_tensor("gwt_sb", [128, NKC, 9], BF16))
        meta_sb = E(nc.sbuf_tensor("meta_sb", [8, 2], F32))
        stats = E(nc.sbuf_tensor("stats", [128, 8 * NT], F32))
        sqs = E(nc.sbuf_tensor("sqs", [128, D], BF16))
        lg = E(nc.sbuf_tensor("lg", [128, NT, 9], F32))
        e9 = E(nc.sbuf_tensor("e9", [128, NT, 9], F32))
        rbufs = {nm: E(nc.sbuf_tensor(nm, [128, NT, 8], F32))
                 for nm in ("rw", "npv", "mk", "ind", "ta", "tb", "mask0")}
        r16 = {nm: E(nc.sbuf_tensor("r_" + nm, [128, NT], F32))
               for nm in ("thr", "athr", "ethr", "z8", "cs", "mult", "valid",
                          "tmp1", "tmp2", "sts")}
        mask_bf = E(nc.sbuf_tensor("mask_bf", [128, NT, 8], BF16))
        mask_sb = E(nc.sbuf_tensor("mask_sb", [128, NT, 8], F32))
        w9 = E(nc.sbuf_tensor("w9", [128, NT, 9], BF16))
        w8 = E(nc.sbuf_tensor("w8", [9, T], BF16))
        c_onesf = E(nc.sbuf_tensor("c_onesf", [1, 128], F32))
        c_sel = E(nc.sbuf_tensor("c_sel", [9, 9 * 128], BF16))
        off_bc = E(nc.sbuf_tensor("off_bc", [128, 128], F32))
        agsb = [E(nc.sbuf_tensor(f"agsb{s}", [8, 8], F32)) for s in (0, 1)]
        cnt_row = E(nc.sbuf_tensor("cnt_row", [1, 8 * NT], F32))
        off_row = E(nc.sbuf_tensor("off_row", [1, 8 * (NT + 2)], F32))
        sval = E(nc.sbuf_tensor("sval", [1, 8], F32))
        biasq = E(nc.sbuf_tensor("biasq", [128, 4], F32))

        op("sync", lambda e: e.dma_start(out=c_id[:], in_=ident_d),
           inc=[("xld", 16)])
        op("sync", lambda e: e.dma_start(out=c_tri[:], in_=tri_d),
           inc=[("xld", 16)])
        op("sync", lambda e: e.dma_start(out=c_ones[:], in_=ones_d),
           inc=[("xld", 16)])
        op("sync", lambda e: e.dma_start(out=meta_sb[:], in_=meta),
           inc=[("xld", 16)])
        op("sync", lambda e: e.dma_start(out=c_onesf[:], in_=onesf_d),
           inc=[("xld", 16)])
        op("sync", lambda e: e.dma_start(out=c_sel[:], in_=sel_d),
           inc=[("xld", 16)])
        op("gpsimd", lambda e: e.dma_start(out=gwt_sb[:], in_=gwt_v),
           inc=[("wld", 16)])
        consts_ld = snap("xld")
        gw_ld = snap("wld")

        op("sync", lambda e: e.dma_start(
            out=x_sb[:], in_=xc_t[0].rearrange("t p d -> p t d")),
            inc=[("xld", 16)])
        x_ld = [snap("xld"), None]

        store_done = [0, 0]
        x_ld8 = []
        gu_cached = [0] * NU
        wd_cached = [0] * NU
        guw_wb = [0, 0]
        wd_wb = [0, 0]

        for SQ in (0, 1):
            ssum = stats[:, 0:NT]
            ssq = stats[:, NT:2 * NT]
            smean = stats[:, 2 * NT:3 * NT]
            srstd = stats[:, 3 * NT:4 * NT]
            svar = stats[:, 4 * NT:5 * NT]
            st1 = stats[:, 5 * NT:6 * NT]

            def emit_ln_half(half, tile_waits):
                h0 = half * (NT // 2)
                sl = slice(h0, h0 + NT // 2)
                sc_b = snap("sc")
                for t in range(h0, h0 + NT // 2):
                    w = tile_waits(t) if tile_waits else ()
                    op("scalar", lambda e, t=t: e.activation(
                        sqs[:], x_sb[:, t, :], AF.Copy,
                        accum_out=ssum[:, t:t + 1]), wait=w)
                    op("scalar", lambda e, t=t: e.activation(
                        sqs[:], x_sb[:, t, :], AF.Square,
                        accum_out=ssq[:, t:t + 1]), inc=[("sc", 1)])
                    if t % 2 == 1:
                        warm(wait=(bwait("pm") if t == h0 + 1 else []) +
                             [("sc", sc_b + (t - h0))])
                m_stats = snap("sc")
                op("vector", lambda e, sl=sl: e.tensor_scalar(
                    smean[:, sl], ssum[:, sl], 1.0 / D, None, ALU.mult),
                   wait=[("sc", m_stats)])
                op("vector", lambda e, sl=sl: e.tensor_tensor(
                    st1[:, sl], smean[:, sl], smean[:, sl], ALU.mult),
                   dep=True)
                op("vector", lambda e, sl=sl: e.tensor_scalar(
                    svar[:, sl], ssq[:, sl], 1.0 / D, None, ALU.mult))
                op("vector", lambda e, sl=sl: e.tensor_tensor(
                    svar[:, sl], svar[:, sl], st1[:, sl], ALU.subtract),
                   dep=True)
                op("vector", lambda e, sl=sl: e.tensor_scalar(
                    svar[:, sl], svar[:, sl], EPS, None, ALU.add),
                   inc=[("v", 1)], dep=True)
                v_var = snap("v")
                op("scalar", lambda e, sl=sl: e.activation(
                    st1[:, sl], svar[:, sl], AF.Ln), wait=[("v", v_var)])
                op("scalar", lambda e, sl=sl: e.activation(
                    srstd[:, sl], st1[:, sl], AF.Exp, scale=-0.5),
                   inc=[("sc", 1)], dep=True)
                m_rstd = snap("sc")
                op("vector", lambda e, sl=sl: e.tensor_tensor(
                    st1[:, sl], srstd[:, sl], srstd[:, sl], ALU.mult),
                   wait=[("sc", m_rstd)])
                op("vector", lambda e, sl=sl: e.tensor_tensor(
                    st1[:, sl], svar[:, sl], st1[:, sl], ALU.mult), dep=True)
                op("vector", lambda e, sl=sl: e.tensor_scalar(
                    st1[:, sl], st1[:, sl], -0.5, 1.5, ALU.mult, ALU.add),
                   dep=True)
                op("vector", lambda e, sl=sl: e.tensor_tensor(
                    srstd[:, sl], srstd[:, sl], st1[:, sl], ALU.mult),
                   dep=True)

            def warm(wait=()):
                op("tensor", lambda e: e.matmul(
                    pm[0:1, 256:384], c_ones[:, 0:1], c_ones[:, 0:128],
                    start=True, stop=True), wait=wait)

            def emit_ln_apply_transpose_half(xt, half_, extra_first_wait):
                h0 = half_ * (NT // 2)
                for t in range(h0, h0 + NT // 2):
                    op("vector", lambda e, t=t: e.tensor_scalar(
                        xt[:, t, :], x_sb[:, t, :], smean[:, t:t + 1],
                        srstd[:, t:t + 1], ALU.subtract, ALU.mult),
                        inc=[("v", 1)] if t == h0 + NT // 2 - 1 else (),
                        dep=(t == h0))
                v_ln = snap("v")
                for grp in range(h0, h0 + NT // 2):
                    half = grp % 2
                    w = list(bwait("t%d" % half))
                    if grp == h0:
                        w.append(("v", v_ln))
                        w += extra_first_wait
                    for j in range(8):
                        t, c = grp, j
                        op("tensor", lambda e, t=t, c=c, half=half: e.transpose(
                            pth(half)[:, c * 128:(c + 1) * 128],
                            xt[:, t, c * 128:(c + 1) * 128], c_id[:]),
                            wait=w if j == 0 else (),
                            inc=[("t", 1)] if j == 7 else ())
                    warm(wait=bwait("pm") if grp == h0 else ())
                    if grp == h0 + NT // 2 - 1:
                        op("tensor", lambda e: e.nop(), inc=[("t", 1)])
                        brel("pm", "t")
                    t_grp = snap("t")
                    for j in range(8):
                        t, c = grp, j
                        op("vector", lambda e, t=t, c=c, half=half: e.tensor_copy(
                            xf[:, c, t * 128:(t + 1) * 128],
                            pth(half)[:, c * 128:(c + 1) * 128]),
                            wait=[("t", t_grp)] if j == 0 else (),
                            inc=[("v", 1)] if j == 7 else ())
                    brel("t%d" % half, "v")
                return snap("v")

            def emit_ln_pipe(xt, tile_waits, extra_first_wait):
                emit_ln_half(0, tile_waits)
                emit_ln_apply_transpose_half(xt, 0, extra_first_wait)
                emit_ln_half(1, tile_waits)
                return emit_ln_apply_transpose_half(xt, 1, ())

            with nc.sbuf_tensor(f"xt{SQ}", [128, NT, D], BF16) as xt:
                if SQ == 0:
                    tw = lambda t: [("xld", x_ld[0])] if t == 0 else ()
                else:
                    tw = lambda t: [("xld", x_ld8[t // 2])]
                v_xf = emit_ln_pipe(xt, tw, [("xld", consts_ld)])
            tap_dump("xf", [("v", v_xf)],
                     [(xf[:].rearrange("p c t -> p (c t)"), 16384, True)], SQ)

            with contextlib.ExitStack() as att:
                A = att.enter_context
                pw = [{k: A(nc.sbuf_tensor(f"pw{SQ}_{i}_{k}",
                                           [128, NKC, 128], BF16))
                       for k in ("q", "k", "v")} for i in range(2)]
                qp = [[A(nc.sbuf_tensor(f"qp{SQ}_{i}_{h}", [128, T], BF16))
                       for h in range(2)] for i in range(2)]
                kp = [[A(nc.sbuf_tensor(f"kp{SQ}_{i}_{h}", [128, T], BF16))
                       for h in range(2)] for i in range(2)]
                vp = [A(nc.sbuf_tensor(f"vp{SQ}_{i}", [128, NT, 130], BF16))
                      for i in range(2)]
                probs = [A(nc.sbuf_tensor(f"probs{SQ}_{h}", [128, NT, 512], BF16))
                         for h in range(2)]
                recip = A(nc.sbuf_tensor(f"recip{SQ}", [128, 8], F32))
                attn_bf = A(nc.sbuf_tensor(f"attnbf{SQ}", [128, 2, 512], BF16))
                ebf_free = [("t", 0), ("t", 0)]

                for i in range(2):
                    op("vector", lambda e, i=i: e.memset(vp[i][:, :, 64:65], 1.0))
                    op("vector", lambda e, i=i: e.memset(vp[i][:, :, 129:130], 1.0))
                    for h in range(2):
                        zs = slice(64, 128) if h == 0 else slice(0, 64)
                        op("vector", lambda e, i=i, h=h, zs=zs: e.memset(
                            qp[i][h][zs, :], 0.0))
                        op("vector", lambda e, i=i, h=h, zs=zs: e.memset(
                            kp[i][h][zs, :], 0.0),
                           inc=[("v", 1)] if (i == 1 and h == 1) else ())
                v_ones = snap("v")

                pw_ld = [0, 0]
                pair_free = [("v", v_ones), ("v", v_ones)]
                prev_tbias = 0
                prev_scdone = 0

                def load_pair(j):
                    i = j % 2
                    w = [pair_free[i]]
                    op("gpsimd", lambda e, j=j, i=i: e.dma_start(
                        out=pw[i]["q"][:],
                        in_=wq_v[:, :, j * 128:(j + 1) * 128]),
                        wait=w, inc=[("wld", 16)])
                    op("gpsimd", lambda e, j=j, i=i: e.dma_start(
                        out=pw[i]["k"][:],
                        in_=wk_v[:, :, j * 128:(j + 1) * 128]),
                        inc=[("wld", 16)])
                    op("gpsimd", lambda e, j=j, i=i: e.dma_start(
                        out=pw[i]["v"][:],
                        in_=wv_v[:, :, j * 128:(j + 1) * 128]),
                        inc=[("wld", 16)])
                    pw_ld[i] = snap("wld")

                def warm_cache_unit(uidx):
                    kind, z = UNITS[uidx]
                    if kind == "d":
                        sg_, su_, sd_ = wg_v[z], wu_v[z], wd_v[z]
                    else:
                        sg_ = wsg_v[:, :, z * 512:(z + 1) * 512]
                        su_ = wsu_v[:, :, z * 512:(z + 1) * 512]
                        sd_ = wsd_v[:, z * 4:(z + 1) * 4, :]
                    op("gpsimd", lambda e, u=uidx, s=sg_: e.dma_start(
                        out=gu_cache.ap()[u, 0], in_=s), inc=[("wld", 16)])
                    op("gpsimd", lambda e, u=uidx, s=su_: e.dma_start(
                        out=gu_cache.ap()[u, 1], in_=s), inc=[("wld", 16)])
                    gu_cached[uidx] = snap("wld")
                    op("gpsimd", lambda e, u=uidx, s=sd_: e.dma_start(
                        out=wd_cache.ap()[u], in_=s), inc=[("wld", 16)])
                    wd_cached[uidx] = snap("wld")

                load_pair(0)

                for j in range(NPAIR):
                    i = j % 2
                    if j + 1 < NPAIR:
                        load_pair(j + 1)
                    if SQ == 0 and j < 6:
                        warm_cache_unit(2 * j)
                        warm_cache_unit(2 * j + 1)
                    first_w = [("wld", pw_ld[i]), ("v", v_xf), pair_free[i]]
                    for which, dst in (("q", qp), ("k", kp)):
                        for qc in range(NQC):
                            bank = "pe" if (qc % 2 == 0) else "pm"
                            pb = pe if bank == "pe" else pm
                            w = list(bwait(bank))
                            if which == "q" and qc == 0:
                                w += first_w
                            for kc in range(NKC):
                                op("tensor", lambda e, i=i, which=which, kc=kc,
                                   qc=qc, pb=pb: e.matmul(
                                       pb[:, 0:512], pw[i][which][:, kc, :],
                                       xf[:, kc, qc * 512:(qc + 1) * 512],
                                       start=(kc == 0), stop=(kc == NKC - 1)),
                                    wait=w if kc == 0 else (),
                                    inc=[("t", 1)] if kc == NKC - 1 else ())
                            t_mm = snap("t")
                            op("vector", lambda e, dst=dst, i=i, qc=qc, pb=pb:
                               e.tensor_copy(
                                   dst[i][0][0:64, qc * 512:(qc + 1) * 512],
                                   pb[0:64, 0:512]),
                               wait=[("t", t_mm)])
                            op("vector", lambda e, dst=dst, i=i, qc=qc, pb=pb:
                               e.tensor_copy(
                                   dst[i][1][64:128, qc * 512:(qc + 1) * 512],
                                   pb[64:128, 0:512]),
                               inc=[("v", 1)])
                            brel(bank, "v")
                    for tt in range(NT):
                        bank = "pe" if (tt % 2 == 0) else "pm"
                        pb = pe if bank == "pe" else pm
                        w = list(bwait(bank))
                        for kc in range(NKC):
                            op("tensor", lambda e, i=i, kc=kc, tt=tt, pb=pb:
                               e.matmul(
                                   pb[:, 0:128],
                                   xf[:, kc, tt * 128:(tt + 1) * 128],
                                   pw[i]["v"][:, kc, :],
                                   start=(kc == 0), stop=(kc == NKC - 1)),
                                wait=w if kc == 0 else (),
                                inc=[("t", 1)] if kc == NKC - 1 else ())
                        t_mm = snap("t")
                        op("vector", lambda e, i=i, tt=tt, pb=pb: e.tensor_copy(
                            vp[i][:, tt, 0:130].rearrange(
                                "p (h x) -> p h x", h=2)[:, :, 0:64],
                            pb[:, 0:128].rearrange("p (h x) -> p h x", h=2)),
                            wait=[("t", t_mm)], inc=[("v", 1)])
                        brel(bank, "v")
                    qkv_done = snap("v")
                    if j == 0:
                        tap_dump("qkv", [("v", qkv_done)],
                                 [(qp[0][0][:, :], 2048, True),
                                  (kp[0][0][:, :], 2048, True),
                                  (vp[0][:].rearrange("p a b -> p (a b)"),
                                   2080, True)], SQ)

                    par = j % 2
                    w0 = list(bwait("s0", "s1")) + [("v", qkv_done)]
                    op("tensor", lambda e, i=i: e.matmul(
                        ps4q(0)[:, 0:512], kp[i][0][:, 0:128],
                        qp[i][0][:, 0:512], start=True, stop=True), wait=w0)
                    op("tensor", lambda e, i=i: e.matmul(
                        ps4q(1)[:, 0:512], kp[i][1][:, 0:128],
                        qp[i][1][:, 0:512], start=True, stop=True),
                       inc=[("t", 1)])
                    t_s00 = snap("t")
                    wv_ = [("t", t_s00)]
                    if prev_tbias:
                        wv_.append(("t", prev_tbias))
                    op("vector", lambda e, par=par: e.tensor_copy(
                        sval[0:1, par * 4:par * 4 + 1], ps4q(0)[0:1, 0:1]),
                       wait=wv_)
                    op("vector", lambda e, par=par: e.tensor_copy(
                        sval[0:1, par * 4 + 1:par * 4 + 2], ps4q(1)[0:1, 0:1]))
                    op("vector", lambda e, par=par: e.tensor_scalar(
                        sval[0:1, par * 4 + 2:par * 4 + 4],
                        sval[0:1, par * 4:par * 4 + 2],
                        -0.125, None, ALU.mult), inc=[("v", 1)], dep=True)
                    v_sval = snap("v")
                    w = list(bwait("pm")) + [("v", v_sval)]
                    op("tensor", lambda e, par=par: e.matmul(
                        pm[:, 0:2], c_onesf[0:1, :],
                        sval[0:1, par * 4 + 2:par * 4 + 4],
                        start=True, stop=True), wait=w, inc=[("t", 1)])
                    t_bias = snap("t")
                    prev_tbias = t_bias
                    wv_ = [("t", t_bias)]
                    if prev_scdone:
                        wv_.append(("sc", prev_scdone))
                    op("vector", lambda e, par=par: e.tensor_copy(
                        biasq[:, par * 2:par * 2 + 2], pm[:, 0:2]),
                       wait=wv_, inc=[("v", 1)])
                    brel("pm", "v")
                    v_bias = snap("v")

                    def emit_qk_group(h, qc, g, extra=()):
                        b0 = (g % 2) * 2
                        w = list(bwait("s%d" % b0, "s%d" % (b0 + 1))) + \
                            list(extra)
                        for kk in range(2):
                            kt = g * 2 + kk
                            op("tensor", lambda e, i=i, h=h, kt=kt,
                               qc=qc, kk=kk, b0=b0: e.matmul(
                                   ps4q(b0 + kk)[:, 0:512],
                                   kp[i][h][:, kt * 128:(kt + 1) * 128],
                                   qp[i][h][:, qc * 512:(qc + 1) * 512],
                                   start=True, stop=True),
                                wait=w if kk == 0 else (),
                                inc=[("t", 1)] if kk == 1 else ())
                        t_qk = snap("t")
                        op("scalar", lambda e, h=h, g=g, par=par, b0=b0:
                           e.activation(
                               probs[h][:, g * 2:(g + 1) * 2, :]
                               .rearrange("p a b -> p (a b)"),
                               ps4[:, b0 * 512:(b0 + 2) * 512], AF.Exp,
                               bias=biasq[:, par * 2 + h:par * 2 + h + 1],
                               scale=0.125),
                            wait=[("t", t_qk)], inc=[("sc", 1)])
                        brel("s%d" % b0, "sc")
                        brel("s%d" % (b0 + 1), "sc")

                    def emit_pv_part(st, g):
                        h, qc, pvb, pbn, sc_h = (st["h"], st["qc"],
                                                 st["pvb"], st["pbn"],
                                                 st["sc"])
                        w = []
                        if g == 0:
                            w = list(bwait(pbn)) + [("sc", sc_h)]
                        for kk in range(4):
                            kt = g * 4 + kk
                            op("tensor", lambda e, i=i, h=h, kt=kt,
                               pvb=pvb: e.matmul(
                                   pvb[0:65, 0:512],
                                   vp[i][:, kt, h * 65:(h + 1) * 65],
                                   probs[h][:, kt, :],
                                   start=(kt == 0), stop=(kt == 15)),
                                wait=w if kk == 0 else (),
                                inc=[("t", 1)] if kt == 15 else ())

                    def emit_pv_epilogue(st):
                        h, qc, pvb, pbn = (st["h"], st["qc"], st["pvb"],
                                           st["pbn"])
                        t_pv = snap("t")
                        par_e = (qc * 2 + h) % 2
                        abf = attn_bf[:, par_e, :]
                        w = [("t", t_pv), ebf_free[par_e]]
                        op("scalar", lambda e, pvb=pvb, abf=abf: e.copy(
                            abf[0:65, :], pvb[0:65, 0:512]),
                           wait=w, inc=[("sc", 1)])
                        sc_ev = snap("sc")
                        brel(pbn, "sc")
                        tb = "t%d" % par_e
                        w2 = list(bwait(tb)) + [("sc", sc_ev)]
                        for qt in range(4):
                            op("tensor", lambda e, abf=abf, qt=qt,
                               par_e=par_e: e.transpose(
                                   pth(par_e)[:, qt * 128:(qt + 1) * 128],
                                   abf[:, qt * 128:(qt + 1) * 128],
                                   c_id[:]),
                                wait=w2 if qt == 0 else (),
                                inc=[("t", 1)] if qt == 3 else ())
                        t_tr = snap("t")
                        ebf_free[par_e] = ("t", t_tr)
                        hh = 2 * j + h
                        for qt in range(4):
                            Tt = qc * 4 + qt
                            rc_ = h * 4 + qt
                            op("vector", lambda e, qt=qt, rc_=rc_,
                               par_e=par_e: e.reciprocal(
                                   recip[:, rc_:rc_ + 1],
                                   pth(par_e)[:, qt * 128 + 64:
                                              qt * 128 + 65]),
                               wait=[("t", t_tr)] if qt == 0 else ())
                            op("vector", lambda e, qt=qt, hh=hh, Tt=Tt,
                               rc_=rc_, par_e=par_e:
                               e.scalar_tensor_tensor(
                                   x_sb[:, Tt, hh * 64:(hh + 1) * 64],
                                   pth(par_e)[:, qt * 128:qt * 128 + 64],
                                   recip[:, rc_:rc_ + 1],
                                   x_sb[:, Tt, hh * 64:(hh + 1) * 64],
                                   ALU.mult, ALU.add), inc=[("v", 1)],
                               dep=True)
                        brel(tb, "v")

                    pending = None
                    for qc in range(NQC):
                        for h in range(2):
                            for g in range(8):
                                emit_qk_group(
                                    h, qc, g,
                                    extra=[("v", v_bias)] if (
                                        qc == 0 and h == 0 and g == 0)
                                    else ())
                                if pending is not None and g % 2 == 1:
                                    emit_pv_part(pending, g // 2)
                            if pending is not None:
                                emit_pv_epilogue(pending)
                            sidx = qc * 2 + h
                            pending = {"h": h, "qc": qc,
                                       "pvb": pe if sidx % 2 == 0 else pm,
                                       "pbn": "pe" if sidx % 2 == 0 else "pm",
                                       "sc": snap("sc")}
                            if j == 0 and qc == 0 and h == 0:
                                tap_dump(
                                    "probs", [("sc", snap("sc"))],
                                    [(probs[0][:].rearrange(
                                        "p a b -> p (a b)"), 8192, True)],
                                    SQ)
                    for g in range(4):
                        emit_pv_part(pending, g)
                    emit_pv_epilogue(pending)
                    prev_scdone = snap("sc")
                    pair_free[i] = ("v", snap("v"))
                    if j == 0:
                        tap_dump("attn1", [pair_free[0]],
                                 [(x_sb[:].rearrange("p a b -> p (a b)"),
                                   16384, False)], SQ)
                v_attn = snap("v")
            tap_dump("xmid", [("v", v_attn)],
                     [(x_sb[:].rearrange("p a b -> p (a b)"), 16384, False)],
                     SQ)

            with nc.sbuf_tensor(f"x2t{SQ}", [128, NT, D], BF16) as xt:
                v_xf2 = emit_ln_pipe(
                    xt, lambda t: [("v", v_attn)] if t % 8 == 0 else (), [])

            for t in range(NT):
                bank = "s0" if t % 2 == 0 else "s1"
                pb = ps4q(0) if t % 2 == 0 else ps4q(1)
                w = list(bwait(bank))
                if t == 0:
                    w += [("v", v_xf2), ("wld", gw_ld)]
                for kc in range(NKC):
                    op("tensor", lambda e, t=t, kc=kc, pb=pb: e.matmul(
                        pb[:, 0:9], xf[:, kc, t * 128:(t + 1) * 128],
                        gwt_sb[:, kc, :], start=(kc == 0),
                        stop=(kc == NKC - 1)),
                        wait=w if kc == 0 else (),
                        inc=[("t", 1)] if kc == NKC - 1 else ())
                t_lgt = snap("t")
                op("vector", lambda e, t=t, pb=pb: e.tensor_copy(
                    lg[:, t, :], pb[:, 0:9]), wait=[("t", t_lgt)],
                   inc=[("v", 1)])
                brel(bank, "v")
            v_lg = snap("v")
            tap_dump("logits", [("v", v_lg)],
                     [(lg[:].rearrange("p a b -> p (a b)"), 144, False)], SQ)

            op("scalar", lambda e: e.activation(
                e9[:].rearrange("p a b -> p (a b)"),
                lg[:].rearrange("p a b -> p (a b)"), AF.Exp),
                wait=[("v", v_lg)], inc=[("sc", 1)])
            sc_e9 = snap("sc")

            l8 = lg[:, :, 0:8]
            E8 = e9[:, :, 0:8]
            rw, npv, mk, ind, ta, tbuf, mask0 = (
                rbufs["rw"], rbufs["npv"], rbufs["mk"], rbufs["ind"],
                rbufs["ta"], rbufs["tb"], rbufs["mask0"])
            R = r16

            def bc8(x16):
                return x16.broadcast_to([128, NT, 8])

            def vop(fn, **kw):
                op("vector", fn, dep=True, **kw)

            v_mix0 = snap("v")
            op("vector", lambda e: e.memset(rw[:], 0.0), wait=[("sc", sc_e9)])
            op("vector", lambda e: e.memset(mask0[:], 0.0))
            op("vector", lambda e: e.memset(npv[:], 1.0))
            op("vector", lambda e: e.reduce_sum(R["z8"][:], E8, axis=AX.X))
            vop(lambda e: e.tensor_scalar(
                R["z8"][:], R["z8"][:], 0.75, None, ALU.mult))
            op("vector", lambda e: e.memset(R["cs"][:], 0.0))
            op("vector", lambda e: e.tensor_scalar(
                ta[:], l8, -1.0, None, ALU.mult))
            vop(lambda e: e.tensor_tensor(ta[:], ta[:], l8, ALU.max))
            for r in range(4):
                vop(lambda e: e.tensor_scalar(
                    mk[:], npv[:], BIG, None, ALU.mult))
                vop(lambda e: e.tensor_scalar(
                    mk[:], mk[:], -BIG, None, ALU.add))
                vop(lambda e: e.tensor_tensor(mk[:], mk[:], l8, ALU.add))
                vop(lambda e: e.reduce_max(R["thr"][:], mk[:], axis=AX.X))
                vop(lambda e: e.tensor_tensor(
                    ind[:], mk[:], bc8(R["thr"]), ALU.is_equal))
                vop(lambda e: e.tensor_copy(tbuf[:], ind[:]))
                vop(lambda e: e.tensor_copy(mk[:], ind[:]))
                vop(lambda e: e.tensor_add(
                    mk[:, :, 1:8], tbuf[:, :, 1:8], tbuf[:, :, 0:7]))
                vop(lambda e: e.tensor_copy(tbuf[:], mk[:]))
                vop(lambda e: e.tensor_add(
                    tbuf[:, :, 2:8], mk[:, :, 2:8], mk[:, :, 0:6]))
                vop(lambda e: e.tensor_copy(mk[:], tbuf[:]))
                vop(lambda e: e.tensor_add(
                    mk[:, :, 4:8], tbuf[:, :, 4:8], tbuf[:, :, 0:4]))
                vop(lambda e: e.tensor_tensor(
                    tbuf[:], mk[:], ind[:], ALU.subtract))
                vop(lambda e: e.tensor_scalar(
                    tbuf[:], tbuf[:], 0.0, None, ALU.is_equal))
                vop(lambda e: e.tensor_tensor(ind[:], ind[:], tbuf[:], ALU.mult))
                vop(lambda e: e.tensor_tensor(tbuf[:], ind[:], E8, ALU.mult))
                vop(lambda e: e.reduce_sum(R["ethr"][:], tbuf[:], axis=AX.X))
                vop(lambda e: e.tensor_scalar(
                    R["athr"][:], R["thr"][:], -1.0, None, ALU.mult))
                vop(lambda e: e.tensor_tensor(
                    R["athr"][:], R["athr"][:], R["thr"][:], ALU.max))
                vop(lambda e: e.tensor_tensor(
                    tbuf[:], ta[:], bc8(R["athr"]), ALU.max))
                vop(lambda e: e.tensor_scalar(
                    tbuf[:], tbuf[:], 0.2, None, ALU.mult))
                vop(lambda e: e.tensor_tensor(
                    mk[:], bc8(R["thr"]), l8, ALU.subtract))
                vop(lambda e: e.tensor_tensor(mk[:], mk[:], tbuf[:], ALU.is_le))
                vop(lambda e: e.tensor_tensor(mk[:], mk[:], npv[:], ALU.mult))
                vop(lambda e: e.tensor_tensor(tbuf[:], mk[:], E8, ALU.mult))
                vop(lambda e: e.reduce_sum(R["tmp1"][:], tbuf[:], axis=AX.X))
                vop(lambda e: e.reciprocal(R["tmp2"][:], R["tmp1"][:]))
                vop(lambda e: e.tensor_tensor(
                    R["mult"][:], R["ethr"][:], R["tmp2"][:], ALU.mult))
                if r == 0:
                    op("vector", lambda e: e.memset(R["valid"][:], 1.0))
                else:
                    vop(lambda e: e.tensor_tensor(
                        R["valid"][:], R["cs"][:], R["z8"][:], ALU.is_lt))
                vop(lambda e: e.tensor_tensor(
                    R["cs"][:], R["cs"][:], R["ethr"][:], ALU.add))
                vop(lambda e: e.tensor_tensor(
                    R["tmp1"][:], R["mult"][:], R["valid"][:], ALU.mult))
                vop(lambda e: e.tensor_tensor(
                    tbuf[:], ind[:], bc8(R["tmp1"]), ALU.mult))
                vop(lambda e: e.tensor_tensor(rw[:], rw[:], tbuf[:], ALU.add))
                vop(lambda e: e.tensor_tensor(
                    tbuf[:], ind[:], bc8(R["valid"]), ALU.mult))
                vop(lambda e: e.tensor_tensor(
                    mask0[:], mask0[:], tbuf[:], ALU.add))
                vop(lambda e: e.tensor_tensor(
                    npv[:], npv[:], ind[:], ALU.subtract))
            vop(lambda e: e.reduce_sum(R["tmp1"][:], rw[:], axis=AX.X))
            vop(lambda e: e.reciprocal(R["tmp2"][:], R["tmp1"][:]))
            vop(lambda e: e.tensor_tensor(
                rw[:], rw[:], bc8(R["tmp2"]), ALU.mult))
            vop(lambda e: e.tensor_copy(mask_bf[:], mask0[:]),
                inc=[("v", 1)])
            v_mix = snap("v")
            step = max(1, (v_mix - v_mix0) // 10)
            for k in range(10):
                warm(wait=(bwait("pm") if k == 0 else []) +
                     [("v", min(v_mix0 + k * step, v_mix))])

            w = list(bwait("pm")) + [("v", v_mix)]
            for t in range(NT):
                op("tensor", lambda e, t=t: e.matmul(
                    pm[0:1, t * 8:(t + 1) * 8], c_ones[:, 0:1],
                    mask_bf[:, t, :], start=True, stop=True),
                    wait=w if t == 0 else (),
                    inc=[("t", 1)] if t == NT - 1 else ())
            t_tot = snap("t")
            op("vector", lambda e: e.tensor_copy(
                cnt_row[0:1, :], pm[0:1, 0:128]),
               wait=[("t", t_tot)], inc=[("v", 1)])
            brel("pm", "v")
            cr = cnt_row[0:1, :].rearrange("p (t x) -> p t x", t=NT)
            op("vector", lambda e: e.reduce_sum(
                off_row[0:1, 0:8], cr.rearrange("p t x -> p x t"), axis=AX.X),
               inc=[("v", 1)], dep=True)
            v_cnts = snap("v")
            ag_state = [0]

            def emit_ag():
                op("gpsimd", lambda e, SQ=SQ: e.dma_start(
                    out=ag_in[SQ].ap(), in_=off_row[0:1, 0:8]),
                    wait=[("v", v_cnts)], inc=[("g", 16)])
                g0 = snap("g")
                op("gpsimd", lambda e, SQ=SQ: e.collective_compute(
                    "AllGather", ALU.bypass,
                    replica_groups=[list(range(NCORE))],
                    ins=[ag_in[SQ].ap()], outs=[ag_out[SQ].ap()]),
                    wait=[("g", g0)], inc=[("g", 1)])
                g1 = snap("g")
                op("gpsimd", lambda e, SQ=SQ: e.dma_start(
                    out=agsb[SQ][:], in_=ag_out[SQ].ap()),
                    wait=[("g", g1)], inc=[("g", 16)])
                ag_state[0] = snap("g")

            def emit_post_ag():
                w = list(bwait("pe")) + [("g", ag_state[0]),
                                     ("xld", consts_ld)]
                if SQ == 0:
                    op("tensor", lambda e: e.matmul(
                        pe[0:1, 0:8], meta_sb[:, 0:1], agsb[0][:],
                        start=True, stop=True), wait=w, inc=[("t", 1)])
                else:
                    op("tensor", lambda e: e.matmul(
                        pe[0:1, 0:8], meta_sb[:, 1:2], agsb[0][:],
                        start=True, stop=False), wait=w)
                    op("tensor", lambda e: e.matmul(
                        pe[0:1, 0:8], meta_sb[:, 0:1], agsb[1][:],
                        start=False, stop=True), inc=[("t", 1)])
                t_off = snap("t")
                op("vector", lambda e: e.tensor_copy(
                    off_row[0:1, 8:16], pe[0:1, 0:8]),
                   wait=[("t", t_off)], inc=[("v", 1)])
                brel("pe", "v")
                op("vector", lambda e: e.tensor_copy(
                    off_row[0:1, 16:24], off_row[0:1, 8:16]), dep=True)
                for t in range(1, NT):
                    op("vector", lambda e, t=t: e.tensor_tensor(
                        off_row[0:1, 16 + t * 8:24 + t * 8],
                        off_row[0:1, 8 + t * 8:16 + t * 8],
                        cnt_row[0:1, (t - 1) * 8:t * 8], ALU.add), dep=True)
                op("vector", lambda e: e.engine_nop(), inc=[("v", 1)], dep=True)
                v_offs = snap("v")
                w = list(bwait("pe")) + [("v", v_offs)]
                op("tensor", lambda e: e.matmul(
                    pe[:, 0:128], c_onesf[0:1, :], off_row[0:1, 16:144],
                    start=True, stop=True), wait=w, inc=[("t", 1)])
                t_obc = snap("t")
                op("vector", lambda e: e.tensor_copy(off_bc[:], pe[:, 0:128]),
                   wait=[("t", t_obc)], inc=[("v", 1)])
                brel("pe", "v")
                v_obc = snap("v")

                for t in range(NT):
                    bank = "s2" if t % 2 == 0 else "s3"
                    pb = ps4q(2) if t % 2 == 0 else ps4q(3)
                    w = list(bwait(bank))
                    if t == 0:
                        w.append(("v", v_obc))
                    op("tensor", lambda e, t=t, pb=pb: e.matmul(
                        pb[:, 0:8], c_tri[:], mask_bf[:, t, :],
                        start=True, stop=True), wait=w, inc=[("t", 1)])
                    t_cum = snap("t")
                    op("vector", lambda e, t=t, pb=pb: e.scalar_tensor_tensor(
                        mk[:, t, :], pb[:, 0:8], 1.0,
                        off_bc[:, t * 8:(t + 1) * 8], ALU.mult, ALU.add),
                        wait=[("t", t_cum)], dep=(t == 0))
                    op("vector", lambda e, t=t: e.scalar_tensor_tensor(
                        mask_sb[:, t, :], mk[:, t, :], CAP, mask0[:, t, :],
                        ALU.is_le, ALU.mult), inc=[("v", 1)], dep=True)
                    brel(bank, "v")
                v_mask = snap("v")

                op("vector", lambda e: e.tensor_tensor(
                    ta[:], e9[:, :, 0:8], mask_sb[:], ALU.mult),
                   wait=[("v", v_mask)], dep=True)
                vop(lambda e: e.reduce_sum(R["tmp1"][:], ta[:], axis=AX.X))
                vop(lambda e: e.tensor_tensor(
                    R["tmp2"][:], R["tmp1"][:], e9[:, :, 8], ALU.add))
                vop(lambda e: e.reciprocal(R["sts"][:], R["tmp2"][:]))
                vop(lambda e: e.tensor_tensor(
                    R["tmp1"][:], R["tmp1"][:], R["sts"][:], ALU.mult))
                vop(lambda e: e.tensor_tensor(
                    tbuf[:], rw[:], mask_sb[:], ALU.mult))
                vop(lambda e: e.tensor_tensor(
                    w9[:, :, 0:8], tbuf[:], bc8(R["tmp1"]), ALU.mult))
                vop(lambda e: e.tensor_tensor(
                    w9[:, :, 8], e9[:, :, 8], R["sts"][:], ALU.mult),
                    inc=[("v", 1)])
                v_w9 = snap("v")

                for t in range(NT):
                    half = t % 2
                    w = list(bwait("t%d" % half))
                    if t == 0:
                        w.append(("v", v_w9))
                    op("tensor", lambda e, t=t, half=half: e.transpose(
                        pth(half)[0:9, 0:128], w9[:, t, :], c_id[:]),
                        wait=w, inc=[("t", 1)])
                    t_w = snap("t")
                    op("vector", lambda e, t=t, half=half: e.tensor_copy(
                        w8[0:9, t * 128:(t + 1) * 128], pth(half)[0:9, 0:128]),
                        wait=[("t", t_w)], inc=[("v", 1)])
                    brel("t%d" % half, "v")
                v_w8 = snap("v")
                tap_dump("route", [("v", v_w8)],
                         [(w9[:].rearrange("p a b -> p (a b)"), 144, True),
                          (mask_sb[:].rearrange("p a b -> p (a b)"), 128, False),
                          (rw[:].rearrange("p a b -> p (a b)"), 128, False),
                          (mask0[:].rearrange("p a b -> p (a b)"), 128, False),
                          (off_bc[:, :], 128, False),
                          (e9[:].rearrange("p a b -> p (a b)"), 144, False)], SQ)

                return v_w8

            with contextlib.ExitStack() as ffn:
                A = ffn.enter_context
                uw = [{"g": A(nc.sbuf_tensor(f"uwg{SQ}_{i}",
                                             [128, NKC, 512], BF16)),
                       "u": A(nc.sbuf_tensor(f"uwu{SQ}_{i}",
                                             [128, NKC, 512], BF16))}
                      for i in range(2)]
                wd = [uw[i]["g"][:].rearrange("p a c -> p (a c)").rearrange(
                    "p (x y) -> p x y", x=4) for i in range(2)]
                ab = [A(nc.sbuf_tensor(f"a{SQ}_{u}", [128, 4, 512], BF16))
                      for u in range(NU)]
                sg = A(nc.sbuf_tensor(f"sg{SQ}", [128, 2, 512], BF16))
                wbc = A(nc.sbuf_tensor(f"wbc{SQ}", [128, 9, 512], BF16))

                guw_ld = [0, 0]
                guw_free = [("v", v_xf2), ("v", v_xf2)]
                wd_ld = [0, 0]
                wd_free = [("v", v_xf2), ("v", v_xf2)]
                sg_free = [("v", 0), ("v", 0)]
                a_guard = ("t", 0)
                wbc_guard = ("v", 0)

                def load_gu(uidx):
                    kind, z = UNITS[uidx]
                    i = uidx % 2
                    w = [guw_free[i]]
                    if guw_wb[i]:
                        w.append(("wld", guw_wb[i]))
                    if gu_cached[uidx]:
                        w.append(("wld", gu_cached[uidx]))
                        op("gpsimd", lambda e, u=uidx, i=i: e.dma_start(
                            out=uw[i]["g"][:], in_=gu_cache.ap()[u, 0]),
                            wait=w, inc=[("wld", 16)])
                        op("gpsimd", lambda e, u=uidx, i=i: e.dma_start(
                            out=uw[i]["u"][:], in_=gu_cache.ap()[u, 1]),
                            inc=[("wld", 16)])
                    else:
                        if kind == "d":
                            srcs = (wg_v[z], wu_v[z])
                        else:
                            srcs = (wsg_v[:, :, z * 512:(z + 1) * 512],
                                    wsu_v[:, :, z * 512:(z + 1) * 512])
                        op("gpsimd", lambda e, s=srcs[0], i=i: e.dma_start(
                            out=uw[i]["g"][:], in_=s),
                            wait=w, inc=[("wld", 16)])
                        ld_g = snap("wld")
                        op("gpsimd", lambda e, s=srcs[1], i=i: e.dma_start(
                            out=uw[i]["u"][:], in_=s),
                            inc=[("wld", 16)])
                        ld_u = snap("wld")
                        op("gpsimd", lambda e, u=uidx, i=i: e.dma_start(
                            out=gu_cache.ap()[u, 0], in_=uw[i]["g"][:]),
                            wait=[("wld", ld_g)], inc=[("wld", 16)])
                        op("gpsimd", lambda e, u=uidx, i=i: e.dma_start(
                            out=gu_cache.ap()[u, 1], in_=uw[i]["u"][:]),
                            wait=[("wld", ld_u)], inc=[("wld", 16)])
                        gu_cached[uidx] = snap("wld")
                        guw_wb[i] = snap("wld")
                    guw_ld[i] = snap("wld")

                def load_wd(uidx):
                    kind, z = UNITS[uidx]
                    i = uidx % 2
                    w = [wd_free[i]]
                    if wd_wb[i]:
                        w.append(("wld", wd_wb[i]))
                    if wd_cached[uidx]:
                        w.append(("wld", wd_cached[uidx]))
                        op("sync", lambda e, u=uidx, i=i: e.dma_start(
                            out=wd[i], in_=wd_cache.ap()[u]),
                            wait=w, inc=[("wld", 16)])
                    else:
                        src = (wd_v[z] if kind == "d"
                               else wsd_v[:, z * 4:(z + 1) * 4, :])
                        op("gpsimd", lambda e, s=src, i=i: e.dma_start(
                            out=wd[i], in_=s),
                            wait=w, inc=[("wld", 16)])
                        ld_d = snap("wld")
                        op("gpsimd", lambda e, u=uidx, i=i: e.dma_start(
                            out=wd_cache.ap()[u], in_=wd[i]),
                            wait=[("wld", ld_d)], inc=[("wld", 16)])
                        wd_cached[uidx] = snap("wld")
                        wd_wb[i] = snap("wld")
                    wd_ld[i] = snap("wld")

                GUB = [("s0", ps4q(0)), ("s1", ps4q(1)),
                       ("pe", pe[:, 0:512]), ("pm", pm[:, 0:512])]
                DBB = {(0, 0): ("s0", ps4q(0)), (0, 1): ("s1", ps4q(1)),
                       (1, 0): ("s2", ps4q(2)), (1, 1): ("s3", ps4q(3))}

                v_w8c = [0]

                def emit_wbc(tq0):
                    for wrow in range(9):
                        bank = "s2" if wrow % 2 == 0 else "s3"
                        pb = ps4q(2) if wrow % 2 == 0 else ps4q(3)
                        w = list(bwait(bank))
                        if wrow == 0:
                            w.append(("v", v_w8c[0]))
                        op("tensor", lambda e, wrow=wrow, tq0=tq0, pb=pb:
                           e.matmul(
                               pb[:, 0:512],
                               c_sel[:, wrow * 128:(wrow + 1) * 128],
                               w8[0:9, tq0:tq0 + 512],
                               start=True, stop=True), wait=w, inc=[("t", 1)])
                        t_wbc = snap("t")
                        op("scalar", lambda e, wrow=wrow, pb=pb: e.copy(
                            wbc[:, wrow, :], pb[:, 0:512]),
                           wait=[("t", t_wbc), wbc_guard], inc=[("sc", 1)])
                        brel(bank, "sc")
                    return snap("sc")

                for qtr in range(4):
                    tq0 = qtr * 512
                    if qtr > 0:
                        sc_wbc = emit_wbc(tq0)

                    load_gu(0)
                    for uidx in range(NU):
                        i = uidx % 2
                        if uidx + 1 < NU:
                            load_gu(uidx + 1)
                        wrow = min(uidx, 8)
                        for ic in range(4):
                            par = (uidx * 4 + ic) % 2
                            gb, gpb = GUB[par * 2]
                            ub, upb = GUB[par * 2 + 1]
                            w = list(bwait(gb)) + [("wld", guw_ld[i])]
                            if uidx == 0 and ic == 0:
                                w.append(a_guard)
                            for kc in range(NKC):
                                op("tensor", lambda e, i=i, ic=ic, kc=kc,
                                   gpb=gpb, tq0=tq0: e.matmul(
                                       gpb, uw[i]["g"][:, kc,
                                                       ic * 128:(ic + 1) * 128],
                                       xf[:, kc, tq0:tq0 + 512],
                                       start=(kc == 0), stop=(kc == NKC - 1)),
                                    wait=w if kc == 0 else (),
                                    inc=[("t", 1)] if kc == NKC - 1 else ())
                            t_g = snap("t")
                            w2 = list(bwait(ub))
                            for kc in range(NKC):
                                op("tensor", lambda e, i=i, ic=ic, kc=kc,
                                   upb=upb, tq0=tq0: e.matmul(
                                       upb, uw[i]["u"][:, kc,
                                                       ic * 128:(ic + 1) * 128],
                                       xf[:, kc, tq0:tq0 + 512],
                                       start=(kc == 0), stop=(kc == NKC - 1)),
                                    wait=w2 if kc == 0 else (),
                                    inc=[("t", 1)] if kc == NKC - 1 else ())
                            t_u = snap("t")
                            op("scalar", lambda e, par=par, gpb=gpb:
                               e.activation(sg[:, par, :], gpb, AF.Silu),
                               wait=[("t", t_g), sg_free[par]],
                               inc=[("sc", 1)])
                            sc_sg = snap("sc")
                            brel(gb, "sc")
                            wv_ = [("t", t_u), ("sc", sc_sg)]
                            op("vector", lambda e, uidx=uidx, ic=ic, par=par,
                               upb=upb: e.tensor_tensor(
                                   ab[uidx][:, ic, :], upb, sg[:, par, :],
                                   ALU.mult), wait=wv_, inc=[("v", 1)])
                            brel(ub, "v")
                            sg_free[par] = ("v", snap("v"))
                        guw_free[i] = ("t", snap("t"))

                    if qtr == 0:
                        emit_ag()
                        v_w8c[0] = emit_post_ag()
                        sc_wbc = emit_wbc(tq0)
                    tt2_done = []
                    for uidx in range(NU):
                        wrow = min(uidx, 8)
                        for ic in range(4):
                            op("vector", lambda e, uidx=uidx, ic=ic,
                               wrow=wrow: e.tensor_tensor(
                                   ab[uidx][:, ic, :], ab[uidx][:, ic, :],
                                   wbc[:, wrow, :], ALU.mult),
                               wait=[("sc", sc_wbc)] if (
                                   uidx == 0 and ic == 0) else (),
                               inc=[("v", 1)])
                        tt2_done.append(snap("v"))
                    v_aall = snap("v")
                    wbc_guard = ("v", v_aall)

                    wd_free[0] = guw_free[0]
                    wd_free[1] = guw_free[1]
                    for eighth in range(2):
                        load_wd(0)
                        for uidx in range(NU):
                            i = uidx % 2
                            if uidx + 1 < NU:
                                load_wd(uidx + 1)
                            nmm = 0
                            for tile in range(2):
                                for half in range(2):
                                    bank, pb = DBB[(tile, half)]
                                    for ic in range(4):
                                        nmm += 1
                                        wd_ = []
                                        if uidx == 0 and ic == 0:
                                            wd_ = list(bwait(bank))
                                        if nmm == 1:
                                            wd_.append(("v", tt2_done[uidx]))
                                            wd_.append(("wld", wd_ld[i]))
                                        op("tensor",
                                           lambda e, uidx=uidx, i=i, ic=ic,
                                           tile=tile, half=half, pb=pb,
                                           eighth=eighth,
                                           s=(uidx == 0 and ic == 0),
                                           p=(uidx == NU - 1 and ic == 3):
                                           e.matmul(
                                               pb,
                                               ab[uidx][:, ic,
                                                        eighth * 256 +
                                                        tile * 128:
                                                        eighth * 256 +
                                                        (tile + 1) * 128],
                                               wd[i][:, ic,
                                                     half * 512:
                                                     (half + 1) * 512],
                                               start=s, stop=p),
                                            wait=wd_,
                                            inc=[("t", 1)] if nmm == 16
                                            else ())
                            t_dn = snap("t")
                            wd_free[i] = ("t", t_dn)
                        t_all = snap("t")
                        for tile in range(2):
                            Tt = qtr * 4 + eighth * 2 + tile
                            for half in range(2):
                                bank, pb = DBB[(tile, half)]
                                op("vector", lambda e, Tt=Tt, half=half,
                                   pb=pb: e.tensor_tensor(
                                       x_sb[:, Tt,
                                            half * 512:(half + 1) * 512],
                                       pb,
                                       x_sb[:, Tt,
                                            half * 512:(half + 1) * 512],
                                       ALU.add),
                                   wait=[("t", t_all)] if (tile == 0 and
                                                           half == 0)
                                   else (), inc=[("v", 1)])
                                brel(bank, "v")
                        v_q = snap("v")
                        q8 = qtr * 2 + eighth
                        op("sync", lambda e, SQ=SQ, q8=q8: e.dma_start(
                            out=out_t[SQ, q8 * 2:(q8 + 1) * 2].rearrange(
                                "t p d -> p t d"),
                            in_=x_sb[:, q8 * 2:(q8 + 1) * 2, :]),
                            wait=[("v", v_q)], inc=[("st", 16)])
                        if SQ == 0:
                            st_now = snap("st")
                            op("sync", lambda e, q8=q8: e.dma_start(
                                out=x_sb[:, q8 * 2:(q8 + 1) * 2, :],
                                in_=xc_t[1, q8 * 2:(q8 + 1) * 2].rearrange(
                                    "t p d -> p t d")),
                                wait=[("st", st_now)], inc=[("xld", 16)])
                            x_ld8.append(snap("xld"))
                    a_guard = ("t", snap("t"))
                    guw_free[0] = a_guard
                    guw_free[1] = a_guard
            store_done[SQ] = snap("st")
            if SQ == 0:
                x_ld[1] = snap("xld")

        op("sync", lambda e: e.nop(), wait=[("st", store_done[1])])

        def play_all(e, eng):
            for rec in ops[eng]:
                for s, v in rec["wait"]:
                    if v > 0:
                        e.wait_ge(sems[s], v)
                ins = rec["fn"](e)
                for s, n in rec["inc"]:
                    ins.then_inc(sems[s], n)

        with nc.Block() as blk:
            @blk.sync
            def _(e):
                play_all(e, "sync")

            @blk.gpsimd
            def _(e):
                play_all(e, "gpsimd")

            @blk.tensor
            def _(e):
                play_all(e, "tensor")

            @blk.vector
            def _(e):
                play_all(e, "vector")

            @blk.scalar
            def _(e):
                play_all(e, "scalar")

    return nc


_NC_CACHE = None


def kernel(L0_attn_x, L0_attn_w_Q, L0_attn_w_K, L0_attn_w_V, L0_ffn_gate_w,
           L0_ffn_dyn_gate_w, L0_ffn_dyn_up_w, L0_ffn_dyn_down_w,
           L0_ffn_sh_gate_w, L0_ffn_sh_up_w, L0_ffn_sh_down_w):
    x = np.ascontiguousarray(np.asarray(L0_attn_x, dtype=np.float32))
    wq = np.ascontiguousarray(np.asarray(L0_attn_w_Q, dtype=np.float32))
    wk = np.ascontiguousarray(np.asarray(L0_attn_w_K, dtype=np.float32))
    wv = np.ascontiguousarray(np.asarray(L0_attn_w_V, dtype=np.float32))
    gwt = np.ascontiguousarray(np.asarray(L0_ffn_gate_w, dtype=np.float32).T)
    wg_t = np.ascontiguousarray(
        np.asarray(L0_ffn_dyn_gate_w, dtype=np.float32).transpose(0, 2, 1))
    wu_t = np.ascontiguousarray(
        np.asarray(L0_ffn_dyn_up_w, dtype=np.float32).transpose(0, 2, 1))
    wd_t = np.ascontiguousarray(
        np.asarray(L0_ffn_dyn_down_w, dtype=np.float32).transpose(0, 2, 1))
    wsg_t = np.ascontiguousarray(
        np.asarray(L0_ffn_sh_gate_w, dtype=np.float32)[0].T)
    wsu_t = np.ascontiguousarray(
        np.asarray(L0_ffn_sh_up_w, dtype=np.float32)[0].T)
    wsd_t = np.ascontiguousarray(
        np.asarray(L0_ffn_sh_down_w, dtype=np.float32)[0].T)

    global _NC_CACHE
    if _NC_CACHE is None or _NC_CACHE[0] != TAP:
        _NC_CACHE = (TAP, build_nc(tap=TAP))
    nc = _NC_CACHE[1]

    in_maps = []
    for c in range(NCORE):
        m = np.zeros((8, 2), dtype=np.float32)
        m[0:c, 0] = 1.0
        m[:, 1] = 1.0
        in_maps.append({
            "xc": np.ascontiguousarray(
                np.concatenate([x[c], x[c + 8]], axis=0)),
            "wq": wq, "wk": wk, "wv": wv, "gwt": gwt,
            "wg_t": wg_t, "wu_t": wu_t, "wd_t": wd_t,
            "wsg_t": wsg_t, "wsu_t": wsu_t, "wsd_t": wsd_t,
            "meta": m,
        })

    res = run_bass_kernel_spmd(nc, in_maps, core_ids=list(range(NCORE)))
    global _LAST_RES
    _LAST_RES = res
    out = np.empty((16, T, D), dtype=np.float32)
    for c in range(NCORE):
        oc = res.results[c]["outc"]
        out[c] = oc[0:T]
        out[c + 8] = oc[T:2 * T]
    return out


_LAST_RES = None


# version 14
# speedup vs baseline: 1.0124x; 1.0124x over previous
import contextlib
import numpy as np
import ml_dtypes
import concourse.bass as bass
import concourse.mybir as mybir
from concourse.bass_utils import run_bass_kernel_spmd

F32 = mybir.dt.float32
BF16 = mybir.dt.bfloat16
AF = mybir.ActivationFunctionType
ALU = mybir.AluOpType
AX = mybir.AxisListType

NCORE = 8
T = 2048
D = 1024
NT = 16
NKC = 8
NQC = 4
NPAIR = 8
E_DYN = 8
CAP = 5120.0
BIG = 1.0e30
EPS = 1e-5

UNITS = [("d", e) for e in range(E_DYN)] + [("s", q) for q in range(4)]
NU = len(UNITS)


TAP = None


def build_nc(tap=None):
    nc = bass.Bass("TRN2", target_bir_lowering=False, debug=False)
    dbg2 = None
    if tap is not None:
        dbg2 = nc.dram_tensor("dbg", [128, 32768], F32,
                              kind="ExternalOutput").ap()

    xc = nc.dram_tensor("xc", [2 * T, D], F32, kind="ExternalInput").ap()
    wq = nc.dram_tensor("wq", [D, D], F32, kind="ExternalInput").ap()
    wk = nc.dram_tensor("wk", [D, D], F32, kind="ExternalInput").ap()
    wv = nc.dram_tensor("wv", [D, D], F32, kind="ExternalInput").ap()
    gwt = nc.dram_tensor("gwt", [D, 9], F32, kind="ExternalInput").ap()
    wg_t = nc.dram_tensor("wg_t", [E_DYN, D, 512], F32, kind="ExternalInput").ap()
    wu_t = nc.dram_tensor("wu_t", [E_DYN, D, 512], F32, kind="ExternalInput").ap()
    wd_t = nc.dram_tensor("wd_t", [E_DYN, 512, D], F32, kind="ExternalInput").ap()
    wsg_t = nc.dram_tensor("wsg_t", [D, 2048], F32, kind="ExternalInput").ap()
    wsu_t = nc.dram_tensor("wsu_t", [D, 2048], F32, kind="ExternalInput").ap()
    wsd_t = nc.dram_tensor("wsd_t", [2048, D], F32, kind="ExternalInput").ap()
    meta = nc.dram_tensor("meta", [8, 2], F32, kind="ExternalInput").ap()
    outc = nc.dram_tensor("outc", [2 * T, D], F32, kind="ExternalOutput").ap()

    gu_cache = nc.dram_tensor("gu_cache", [NU, 2, 128, NKC, 512], BF16,
                              kind="Internal")
    wd_cache = nc.dram_tensor("wd_cache", [NU, 128, 4, D], BF16,
                              kind="Internal")
    ag_in = [nc.dram_tensor(f"ag_in{s}", [1, 8], F32, kind="Internal")
             for s in (0, 1)]
    ag_out = [nc.dram_tensor(f"ag_out{s}", [8, 8], F32, kind="Internal",
                             addr_space="Shared") for s in (0, 1)]

    ident_d = nc.inline_tensor(np.eye(128, dtype=ml_dtypes.bfloat16),
                               name="ident").ap()
    tri_d = nc.inline_tensor(
        np.triu(np.ones((128, 128))).astype(ml_dtypes.bfloat16), name="tri").ap()
    ones_d = nc.inline_tensor(
        np.ones((128, 128), dtype=ml_dtypes.bfloat16), name="ones").ap()
    onesf_d = nc.inline_tensor(
        np.ones((1, 128), dtype=np.float32), name="onesf").ap()
    sel_np = np.zeros((9, 9 * 128), dtype=ml_dtypes.bfloat16)
    for _e in range(9):
        sel_np[_e, _e * 128:(_e + 1) * 128] = 1
    sel_d = nc.inline_tensor(sel_np, name="sel").ap()

    xc_t = xc.rearrange("(s t p) d -> s t p d", s=2, p=128)
    wq_v = wq.rearrange("(kc p) o -> p kc o", p=128)
    wk_v = wk.rearrange("(kc p) o -> p kc o", p=128)
    wv_v = wv.rearrange("(kc p) o -> p kc o", p=128)
    gwt_v = gwt.rearrange("(kc p) o -> p kc o", p=128)
    wg_v = wg_t.rearrange("e (kc p) i -> e p kc i", p=128)
    wu_v = wu_t.rearrange("e (kc p) i -> e p kc i", p=128)
    wd_v = wd_t.rearrange("e (ic p) h -> e p ic h", p=128)
    wsg_v = wsg_t.rearrange("(kc p) i -> p kc i", p=128)
    wsu_v = wsu_t.rearrange("(kc p) i -> p kc i", p=128)
    wsd_v = wsd_t.rearrange("(ic p) h -> p ic h", p=128)
    out_t = outc.rearrange("(s t p) d -> s t p d", s=2, p=128)

    ops = {k: [] for k in ("tensor", "vector", "scalar", "sync", "gpsimd")}
    cnt = {}
    sem_names = ["xld", "wld", "t", "v", "sc", "g", "st"]
    for s in sem_names:
        cnt[s] = 0
    sems = {}

    ENG_SEM = {"tensor": "t", "vector": "v", "scalar": "sc", "gpsimd": "g",
               "sync": "st"}

    def op(eng, fn, wait=(), inc=(), dep=False):
        w = list(wait)
        if dep and ops[eng]:
            sem = ENG_SEM[eng]
            prev = ops[eng][-1]
            if not any(s == sem for s, _ in prev["inc"]):
                prev["inc"].append((sem, 1))
                cnt[sem] += 1
            w.append((sem, cnt[sem]))
        rec = {"fn": fn, "wait": w, "inc": list(inc)}
        for s, n in inc:
            cnt[s] += n
        ops[eng].append(rec)

    def snap(s):
        return cnt[s]

    bankw = {}

    def bwait(*banks):
        return [bankw[b] for b in banks if b in bankw]

    def brel(b, sem):
        bankw[b] = (sem, cnt[sem])

    def tap_dump(name, waits, items, sq):
        if tap != name or sq != 0:
            return
        off = 0
        for ap_in, n, cast in items:
            eng = "gpsimd" if cast else "sync"
            op(eng, lambda e, ap_in=ap_in, off=off, n=n: e.dma_start(
                out=dbg2[0:ap_in.shape[0], off:off + n], in_=ap_in),
                wait=waits, inc=[("st", 16)])
            off += n

    with contextlib.ExitStack() as es:
        E = es.enter_context
        for s in sem_names:
            sems[s] = E(nc.semaphore(s))

        ps4 = E(nc.psum_tensor("ps4", [128, 2048], F32))
        pe = E(nc.psum_tensor("pe", [128, 512], F32))
        pm = E(nc.psum_tensor("pm", [128, 512], F32))
        pt = E(nc.psum_tensor("pt", [128, 2048], BF16))

        def ps4q(i):
            return ps4[:, i * 512:(i + 1) * 512]

        def pth(i):
            return pt[:, i * 1024:(i + 1) * 1024]

        x_sb = E(nc.sbuf_tensor("x_sb", [128, NT, D], F32))
        xf = E(nc.sbuf_tensor("xf", [128, NKC, T], BF16))
        c_id = E(nc.sbuf_tensor("c_id", [128, 128], BF16))
        c_tri = E(nc.sbuf_tensor("c_tri", [128, 128], BF16))
        c_ones = E(nc.sbuf_tensor("c_ones", [128, 128], BF16))
        gwt_sb = E(nc.sbuf_tensor("gwt_sb", [128, NKC, 9], BF16))
        meta_sb = E(nc.sbuf_tensor("meta_sb", [8, 2], F32))
        stats = E(nc.sbuf_tensor("stats", [128, 8 * NT], F32))
        sqs = E(nc.sbuf_tensor("sqs", [128, D], BF16))
        lg = E(nc.sbuf_tensor("lg", [128, NT, 9], F32))
        e9 = E(nc.sbuf_tensor("e9", [128, NT, 9], F32))
        rbufs = {nm: E(nc.sbuf_tensor(nm, [128, NT, 8], F32))
                 for nm in ("rw", "npv", "mk", "ind", "ta", "tb", "mask0")}
        r16 = {nm: E(nc.sbuf_tensor("r_" + nm, [128, NT], F32))
               for nm in ("thr", "athr", "ethr", "z8", "cs", "mult", "valid",
                          "tmp1", "tmp2", "sts")}
        mask_bf = E(nc.sbuf_tensor("mask_bf", [128, NT, 8], BF16))
        mask_sb = E(nc.sbuf_tensor("mask_sb", [128, NT, 8], F32))
        w9 = E(nc.sbuf_tensor("w9", [128, NT, 9], BF16))
        w8 = E(nc.sbuf_tensor("w8", [9, T], BF16))
        c_onesf = E(nc.sbuf_tensor("c_onesf", [1, 128], F32))
        c_sel = E(nc.sbuf_tensor("c_sel", [9, 9 * 128], BF16))
        off_bc = E(nc.sbuf_tensor("off_bc", [128, 128], F32))
        agsb = [E(nc.sbuf_tensor(f"agsb{s}", [8, 8], F32)) for s in (0, 1)]
        cnt_row = E(nc.sbuf_tensor("cnt_row", [1, 8 * NT], F32))
        off_row = E(nc.sbuf_tensor("off_row", [1, 8 * (NT + 2)], F32))
        sval = E(nc.sbuf_tensor("sval", [1, 8], F32))
        biasq = E(nc.sbuf_tensor("biasq", [128, 4], F32))

        op("sync", lambda e: e.dma_start(out=c_id[:], in_=ident_d),
           inc=[("xld", 16)])
        op("sync", lambda e: e.dma_start(out=c_tri[:], in_=tri_d),
           inc=[("xld", 16)])
        op("sync", lambda e: e.dma_start(out=c_ones[:], in_=ones_d),
           inc=[("xld", 16)])
        op("sync", lambda e: e.dma_start(out=meta_sb[:], in_=meta),
           inc=[("xld", 16)])
        op("sync", lambda e: e.dma_start(out=c_onesf[:], in_=onesf_d),
           inc=[("xld", 16)])
        op("sync", lambda e: e.dma_start(out=c_sel[:], in_=sel_d),
           inc=[("xld", 16)])
        op("gpsimd", lambda e: e.dma_start(out=gwt_sb[:], in_=gwt_v),
           inc=[("wld", 16)])
        consts_ld = snap("xld")
        gw_ld = snap("wld")

        op("sync", lambda e: e.dma_start(
            out=x_sb[:], in_=xc_t[0].rearrange("t p d -> p t d")),
            inc=[("xld", 16)])
        x_ld = [snap("xld"), None]

        store_done = [0, 0]
        x_ld8 = []
        gu_cached = [0] * NU
        wd_cached = [0] * NU
        guw_wb = [0, 0]
        wd_wb = [0, 0]

        for SQ in (0, 1):
            ssum = stats[:, 0:NT]
            ssq = stats[:, NT:2 * NT]
            smean = stats[:, 2 * NT:3 * NT]
            srstd = stats[:, 3 * NT:4 * NT]
            svar = stats[:, 4 * NT:5 * NT]
            st1 = stats[:, 5 * NT:6 * NT]

            def emit_ln_half(half, tile_waits):
                h0 = half * (NT // 2)
                sl = slice(h0, h0 + NT // 2)
                sc_b = snap("sc")
                for t in range(h0, h0 + NT // 2):
                    w = tile_waits(t) if tile_waits else ()
                    op("scalar", lambda e, t=t: e.activation(
                        sqs[:], x_sb[:, t, :], AF.Copy,
                        accum_out=ssum[:, t:t + 1]), wait=w)
                    op("scalar", lambda e, t=t: e.activation(
                        sqs[:], x_sb[:, t, :], AF.Square,
                        accum_out=ssq[:, t:t + 1]), inc=[("sc", 1)])
                    if t % 2 == 1:
                        warm(wait=(bwait("pm") if t == h0 + 1 else []) +
                             [("sc", sc_b + (t - h0))])
                m_stats = snap("sc")
                op("vector", lambda e, sl=sl: e.tensor_scalar(
                    smean[:, sl], ssum[:, sl], 1.0 / D, None, ALU.mult),
                   wait=[("sc", m_stats)])
                op("vector", lambda e, sl=sl: e.tensor_tensor(
                    st1[:, sl], smean[:, sl], smean[:, sl], ALU.mult),
                   dep=True)
                op("vector", lambda e, sl=sl: e.tensor_scalar(
                    svar[:, sl], ssq[:, sl], 1.0 / D, None, ALU.mult))
                op("vector", lambda e, sl=sl: e.tensor_tensor(
                    svar[:, sl], svar[:, sl], st1[:, sl], ALU.subtract),
                   dep=True)
                op("vector", lambda e, sl=sl: e.tensor_scalar(
                    svar[:, sl], svar[:, sl], EPS, None, ALU.add),
                   inc=[("v", 1)], dep=True)
                v_var = snap("v")
                op("scalar", lambda e, sl=sl: e.activation(
                    st1[:, sl], svar[:, sl], AF.Ln), wait=[("v", v_var)])
                op("scalar", lambda e, sl=sl: e.activation(
                    srstd[:, sl], st1[:, sl], AF.Exp, scale=-0.5),
                   inc=[("sc", 1)], dep=True)
                m_rstd = snap("sc")
                op("vector", lambda e, sl=sl: e.tensor_tensor(
                    st1[:, sl], srstd[:, sl], srstd[:, sl], ALU.mult),
                   wait=[("sc", m_rstd)])
                op("vector", lambda e, sl=sl: e.tensor_tensor(
                    st1[:, sl], svar[:, sl], st1[:, sl], ALU.mult), dep=True)
                op("vector", lambda e, sl=sl: e.tensor_scalar(
                    st1[:, sl], st1[:, sl], -0.5, 1.5, ALU.mult, ALU.add),
                   dep=True)
                op("vector", lambda e, sl=sl: e.tensor_tensor(
                    srstd[:, sl], srstd[:, sl], st1[:, sl], ALU.mult),
                   dep=True)

            def warm(wait=()):
                op("tensor", lambda e: e.matmul(
                    pm[0:1, 256:384], c_ones[:, 0:1], c_ones[:, 0:128],
                    start=True, stop=True), wait=wait)

            def emit_ln_apply_transpose_half(xt, half_, extra_first_wait):
                h0 = half_ * (NT // 2)
                for t in range(h0, h0 + NT // 2):
                    op("vector", lambda e, t=t: e.tensor_scalar(
                        xt[:, t, :], x_sb[:, t, :], smean[:, t:t + 1],
                        srstd[:, t:t + 1], ALU.subtract, ALU.mult),
                        inc=[("v", 1)] if t == h0 + NT // 2 - 1 else (),
                        dep=(t == h0))
                v_ln = snap("v")
                for grp in range(h0, h0 + NT // 2):
                    half = grp % 2
                    w = list(bwait("t%d" % half))
                    if grp == h0:
                        w.append(("v", v_ln))
                        w += extra_first_wait
                    for j in range(8):
                        t, c = grp, j
                        op("tensor", lambda e, t=t, c=c, half=half: e.transpose(
                            pth(half)[:, c * 128:(c + 1) * 128],
                            xt[:, t, c * 128:(c + 1) * 128], c_id[:]),
                            wait=w if j == 0 else (),
                            inc=[("t", 1)] if j == 7 else ())
                    warm(wait=bwait("pm") if grp == h0 else ())
                    if grp == h0 + NT // 2 - 1:
                        op("tensor", lambda e: e.nop(), inc=[("t", 1)])
                        brel("pm", "t")
                    t_grp = snap("t")
                    for j in range(8):
                        t, c = grp, j
                        op("vector", lambda e, t=t, c=c, half=half: e.tensor_copy(
                            xf[:, c, t * 128:(t + 1) * 128],
                            pth(half)[:, c * 128:(c + 1) * 128]),
                            wait=[("t", t_grp)] if j == 0 else (),
                            inc=[("v", 1)] if j == 7 else ())
                    brel("t%d" % half, "v")
                return snap("v")

            def emit_ln_pipe(xt, tile_waits, extra_first_wait):
                emit_ln_half(0, tile_waits)
                emit_ln_apply_transpose_half(xt, 0, extra_first_wait)
                emit_ln_half(1, tile_waits)
                return emit_ln_apply_transpose_half(xt, 1, ())

            with nc.sbuf_tensor(f"xt{SQ}", [128, NT, D], BF16) as xt:
                if SQ == 0:
                    tw = lambda t: [("xld", x_ld[0])] if t == 0 else ()
                else:
                    tw = lambda t: [("xld", x_ld8[t // 2])]
                v_xf = emit_ln_pipe(xt, tw, [("xld", consts_ld)])
            tap_dump("xf", [("v", v_xf)],
                     [(xf[:].rearrange("p c t -> p (c t)"), 16384, True)], SQ)

            with contextlib.ExitStack() as att:
                A = att.enter_context
                pw = [{k: A(nc.sbuf_tensor(f"pw{SQ}_{i}_{k}",
                                           [128, NKC, 128], BF16))
                       for k in ("q", "k", "v")} for i in range(2)]
                qp = [[A(nc.sbuf_tensor(f"qp{SQ}_{i}_{h}", [128, T], BF16))
                       for h in range(2)] for i in range(2)]
                kp = [[A(nc.sbuf_tensor(f"kp{SQ}_{i}_{h}", [128, T], BF16))
                       for h in range(2)] for i in range(2)]
                vp = [A(nc.sbuf_tensor(f"vp{SQ}_{i}", [128, NT, 130], BF16))
                      for i in range(2)]
                probs = [A(nc.sbuf_tensor(f"probs{SQ}_{h}", [128, NT, 512], BF16))
                         for h in range(2)]
                recip = A(nc.sbuf_tensor(f"recip{SQ}", [128, 8], F32))
                attn_bf = A(nc.sbuf_tensor(f"attnbf{SQ}", [128, 2, 512], BF16))
                ebf_free = [("t", 0), ("t", 0)]

                for i in range(2):
                    op("vector", lambda e, i=i: e.memset(vp[i][:, :, 64:65], 1.0))
                    op("vector", lambda e, i=i: e.memset(vp[i][:, :, 129:130], 1.0))
                    for h in range(2):
                        zs = slice(64, 128) if h == 0 else slice(0, 64)
                        op("vector", lambda e, i=i, h=h, zs=zs: e.memset(
                            qp[i][h][zs, :], 0.0))
                        op("vector", lambda e, i=i, h=h, zs=zs: e.memset(
                            kp[i][h][zs, :], 0.0),
                           inc=[("v", 1)] if (i == 1 and h == 1) else ())
                v_ones = snap("v")

                pw_ld = [0, 0]
                pair_free = [("v", v_ones), ("v", v_ones)]
                prev_tbias = 0
                prev_scdone = 0

                def load_pair(j):
                    i = j % 2
                    w = [pair_free[i]]
                    op("gpsimd", lambda e, j=j, i=i: e.dma_start(
                        out=pw[i]["q"][:],
                        in_=wq_v[:, :, j * 128:(j + 1) * 128]),
                        wait=w, inc=[("wld", 16)])
                    op("gpsimd", lambda e, j=j, i=i: e.dma_start(
                        out=pw[i]["k"][:],
                        in_=wk_v[:, :, j * 128:(j + 1) * 128]),
                        inc=[("wld", 16)])
                    op("gpsimd", lambda e, j=j, i=i: e.dma_start(
                        out=pw[i]["v"][:],
                        in_=wv_v[:, :, j * 128:(j + 1) * 128]),
                        inc=[("wld", 16)])
                    pw_ld[i] = snap("wld")

                def warm_cache_unit(uidx):
                    kind, z = UNITS[uidx]
                    if kind == "d":
                        sg_, su_, sd_ = wg_v[z], wu_v[z], wd_v[z]
                    else:
                        sg_ = wsg_v[:, :, z * 512:(z + 1) * 512]
                        su_ = wsu_v[:, :, z * 512:(z + 1) * 512]
                        sd_ = wsd_v[:, z * 4:(z + 1) * 4, :]
                    op("gpsimd", lambda e, u=uidx, s=sg_: e.dma_start(
                        out=gu_cache.ap()[u, 0], in_=s), inc=[("wld", 16)])
                    op("gpsimd", lambda e, u=uidx, s=su_: e.dma_start(
                        out=gu_cache.ap()[u, 1], in_=s), inc=[("wld", 16)])
                    gu_cached[uidx] = snap("wld")
                    op("gpsimd", lambda e, u=uidx, s=sd_: e.dma_start(
                        out=wd_cache.ap()[u], in_=s), inc=[("wld", 16)])
                    wd_cached[uidx] = snap("wld")

                load_pair(0)

                for j in range(NPAIR):
                    i = j % 2
                    if j + 1 < NPAIR:
                        load_pair(j + 1)
                    if SQ == 0 and j < 6:
                        warm_cache_unit(2 * j)
                        warm_cache_unit(2 * j + 1)
                    first_w = [("wld", pw_ld[i]), ("v", v_xf), pair_free[i]]
                    for which, dst in (("q", qp), ("k", kp)):
                        for qc in range(NQC):
                            bank = "pe" if (qc % 2 == 0) else "pm"
                            pb = pe if bank == "pe" else pm
                            w = list(bwait(bank))
                            if which == "q" and qc == 0:
                                w += first_w
                            for kc in range(NKC):
                                op("tensor", lambda e, i=i, which=which, kc=kc,
                                   qc=qc, pb=pb: e.matmul(
                                       pb[:, 0:512], pw[i][which][:, kc, :],
                                       xf[:, kc, qc * 512:(qc + 1) * 512],
                                       start=(kc == 0), stop=(kc == NKC - 1)),
                                    wait=w if kc == 0 else (),
                                    inc=[("t", 1)] if kc == NKC - 1 else ())
                            t_mm = snap("t")
                            op("vector", lambda e, dst=dst, i=i, qc=qc, pb=pb:
                               e.tensor_copy(
                                   dst[i][0][0:64, qc * 512:(qc + 1) * 512],
                                   pb[0:64, 0:512]),
                               wait=[("t", t_mm)])
                            op("vector", lambda e, dst=dst, i=i, qc=qc, pb=pb:
                               e.tensor_copy(
                                   dst[i][1][64:128, qc * 512:(qc + 1) * 512],
                                   pb[64:128, 0:512]),
                               inc=[("v", 1)])
                            brel(bank, "v")
                    for tt in range(NT):
                        bank = "pe" if (tt % 2 == 0) else "pm"
                        pb = pe if bank == "pe" else pm
                        w = list(bwait(bank))
                        for kc in range(NKC):
                            op("tensor", lambda e, i=i, kc=kc, tt=tt, pb=pb:
                               e.matmul(
                                   pb[:, 0:128],
                                   xf[:, kc, tt * 128:(tt + 1) * 128],
                                   pw[i]["v"][:, kc, :],
                                   start=(kc == 0), stop=(kc == NKC - 1)),
                                wait=w if kc == 0 else (),
                                inc=[("t", 1)] if kc == NKC - 1 else ())
                        t_mm = snap("t")
                        op("vector", lambda e, i=i, tt=tt, pb=pb: e.tensor_copy(
                            vp[i][:, tt, 0:130].rearrange(
                                "p (h x) -> p h x", h=2)[:, :, 0:64],
                            pb[:, 0:128].rearrange("p (h x) -> p h x", h=2)),
                            wait=[("t", t_mm)], inc=[("v", 1)])
                        brel(bank, "v")
                    qkv_done = snap("v")
                    if j == 0:
                        tap_dump("qkv", [("v", qkv_done)],
                                 [(qp[0][0][:, :], 2048, True),
                                  (kp[0][0][:, :], 2048, True),
                                  (vp[0][:].rearrange("p a b -> p (a b)"),
                                   2080, True)], SQ)

                    par = j % 2
                    w0 = list(bwait("s0", "s1")) + [("v", qkv_done)]
                    op("tensor", lambda e, i=i: e.matmul(
                        ps4q(0)[:, 0:512], kp[i][0][:, 0:128],
                        qp[i][0][:, 0:512], start=True, stop=True), wait=w0)
                    op("tensor", lambda e, i=i: e.matmul(
                        ps4q(1)[:, 0:512], kp[i][1][:, 0:128],
                        qp[i][1][:, 0:512], start=True, stop=True),
                       inc=[("t", 1)])
                    t_s00 = snap("t")
                    wv_ = [("t", t_s00)]
                    if prev_tbias:
                        wv_.append(("t", prev_tbias))
                    op("vector", lambda e, par=par: e.tensor_copy(
                        sval[0:1, par * 4:par * 4 + 1], ps4q(0)[0:1, 0:1]),
                       wait=wv_)
                    op("vector", lambda e, par=par: e.tensor_copy(
                        sval[0:1, par * 4 + 1:par * 4 + 2], ps4q(1)[0:1, 0:1]))
                    op("vector", lambda e, par=par: e.tensor_scalar(
                        sval[0:1, par * 4 + 2:par * 4 + 4],
                        sval[0:1, par * 4:par * 4 + 2],
                        -0.125, None, ALU.mult), inc=[("v", 1)], dep=True)
                    v_sval = snap("v")
                    w = list(bwait("pm")) + [("v", v_sval)]
                    op("tensor", lambda e, par=par: e.matmul(
                        pm[:, 0:2], c_onesf[0:1, :],
                        sval[0:1, par * 4 + 2:par * 4 + 4],
                        start=True, stop=True), wait=w, inc=[("t", 1)])
                    t_bias = snap("t")
                    prev_tbias = t_bias
                    wv_ = [("t", t_bias)]
                    if prev_scdone:
                        wv_.append(("sc", prev_scdone))
                    op("vector", lambda e, par=par: e.tensor_copy(
                        biasq[:, par * 2:par * 2 + 2], pm[:, 0:2]),
                       wait=wv_, inc=[("v", 1)])
                    brel("pm", "v")
                    v_bias = snap("v")

                    def emit_qk_group(h, qc, g, extra=()):
                        b0 = (g % 2) * 2
                        w = list(bwait("s%d" % b0, "s%d" % (b0 + 1))) + \
                            list(extra)
                        for kk in range(2):
                            kt = g * 2 + kk
                            op("tensor", lambda e, i=i, h=h, kt=kt,
                               qc=qc, kk=kk, b0=b0: e.matmul(
                                   ps4q(b0 + kk)[:, 0:512],
                                   kp[i][h][:, kt * 128:(kt + 1) * 128],
                                   qp[i][h][:, qc * 512:(qc + 1) * 512],
                                   start=True, stop=True),
                                wait=w if kk == 0 else (),
                                inc=[("t", 1)] if kk == 1 else ())
                        t_qk = snap("t")
                        op("scalar", lambda e, h=h, g=g, par=par, b0=b0:
                           e.activation(
                               probs[h][:, g * 2:(g + 1) * 2, :]
                               .rearrange("p a b -> p (a b)"),
                               ps4[:, b0 * 512:(b0 + 2) * 512], AF.Exp,
                               bias=biasq[:, par * 2 + h:par * 2 + h + 1],
                               scale=0.125),
                            wait=[("t", t_qk)], inc=[("sc", 1)])
                        brel("s%d" % b0, "sc")
                        brel("s%d" % (b0 + 1), "sc")

                    def emit_pv_part(st, g):
                        h, qc, pvb, pbn, sc_h = (st["h"], st["qc"],
                                                 st["pvb"], st["pbn"],
                                                 st["sc"])
                        w = []
                        if g == 0:
                            w = list(bwait(pbn)) + [("sc", sc_h)]
                        for kk in range(4):
                            kt = g * 4 + kk
                            op("tensor", lambda e, i=i, h=h, kt=kt,
                               pvb=pvb: e.matmul(
                                   pvb[0:65, 0:512],
                                   vp[i][:, kt, h * 65:(h + 1) * 65],
                                   probs[h][:, kt, :],
                                   start=(kt == 0), stop=(kt == 15)),
                                wait=w if kk == 0 else (),
                                inc=[("t", 1)] if kt == 15 else ())

                    def emit_pv_epilogue(st):
                        h, qc, pvb, pbn = (st["h"], st["qc"], st["pvb"],
                                           st["pbn"])
                        t_pv = snap("t")
                        par_e = (qc * 2 + h) % 2
                        abf = attn_bf[:, par_e, :]
                        w = [("t", t_pv), ebf_free[par_e]]
                        op("scalar", lambda e, pvb=pvb, abf=abf: e.copy(
                            abf[0:65, :], pvb[0:65, 0:512]),
                           wait=w, inc=[("sc", 1)])
                        sc_ev = snap("sc")
                        brel(pbn, "sc")
                        tb = "t%d" % par_e
                        w2 = list(bwait(tb)) + [("sc", sc_ev)]
                        for qt in range(4):
                            op("tensor", lambda e, abf=abf, qt=qt,
                               par_e=par_e: e.transpose(
                                   pth(par_e)[:, qt * 128:(qt + 1) * 128],
                                   abf[:, qt * 128:(qt + 1) * 128],
                                   c_id[:]),
                                wait=w2 if qt == 0 else (),
                                inc=[("t", 1)] if qt == 3 else ())
                        t_tr = snap("t")
                        ebf_free[par_e] = ("t", t_tr)
                        hh = 2 * j + h
                        for qt in range(4):
                            Tt = qc * 4 + qt
                            rc_ = h * 4 + qt
                            op("vector", lambda e, qt=qt, rc_=rc_,
                               par_e=par_e: e.reciprocal(
                                   recip[:, rc_:rc_ + 1],
                                   pth(par_e)[:, qt * 128 + 64:
                                              qt * 128 + 65]),
                               wait=[("t", t_tr)] if qt == 0 else ())
                            op("vector", lambda e, qt=qt, hh=hh, Tt=Tt,
                               rc_=rc_, par_e=par_e:
                               e.scalar_tensor_tensor(
                                   x_sb[:, Tt, hh * 64:(hh + 1) * 64],
                                   pth(par_e)[:, qt * 128:qt * 128 + 64],
                                   recip[:, rc_:rc_ + 1],
                                   x_sb[:, Tt, hh * 64:(hh + 1) * 64],
                                   ALU.mult, ALU.add), inc=[("v", 1)],
                               dep=True)
                        brel(tb, "v")

                    pending = None
                    for qc in range(NQC):
                        for h in range(2):
                            for g in range(8):
                                emit_qk_group(
                                    h, qc, g,
                                    extra=[("v", v_bias)] if (
                                        qc == 0 and h == 0 and g == 0)
                                    else ())
                                if pending is not None and g % 2 == 1:
                                    emit_pv_part(pending, g // 2)
                            if pending is not None:
                                emit_pv_epilogue(pending)
                            sidx = qc * 2 + h
                            pending = {"h": h, "qc": qc,
                                       "pvb": pe if sidx % 2 == 0 else pm,
                                       "pbn": "pe" if sidx % 2 == 0 else "pm",
                                       "sc": snap("sc")}
                            if j == 0 and qc == 0 and h == 0:
                                tap_dump(
                                    "probs", [("sc", snap("sc"))],
                                    [(probs[0][:].rearrange(
                                        "p a b -> p (a b)"), 8192, True)],
                                    SQ)
                    for g in range(4):
                        emit_pv_part(pending, g)
                    emit_pv_epilogue(pending)
                    prev_scdone = snap("sc")
                    pair_free[i] = ("v", snap("v"))
                    if j == 0:
                        tap_dump("attn1", [pair_free[0]],
                                 [(x_sb[:].rearrange("p a b -> p (a b)"),
                                   16384, False)], SQ)
                v_attn = snap("v")
            tap_dump("xmid", [("v", v_attn)],
                     [(x_sb[:].rearrange("p a b -> p (a b)"), 16384, False)],
                     SQ)

            with nc.sbuf_tensor(f"x2t{SQ}", [128, NT, D], BF16) as xt:
                v_xf2 = emit_ln_pipe(
                    xt, lambda t: [("v", v_attn)] if t % 8 == 0 else (), [])

            for t in range(NT):
                bank = "s0" if t % 2 == 0 else "s1"
                pb = ps4q(0) if t % 2 == 0 else ps4q(1)
                w = list(bwait(bank))
                if t == 0:
                    w += [("v", v_xf2), ("wld", gw_ld)]
                for kc in range(NKC):
                    op("tensor", lambda e, t=t, kc=kc, pb=pb: e.matmul(
                        pb[:, 0:9], xf[:, kc, t * 128:(t + 1) * 128],
                        gwt_sb[:, kc, :], start=(kc == 0),
                        stop=(kc == NKC - 1)),
                        wait=w if kc == 0 else (),
                        inc=[("t", 1)] if kc == NKC - 1 else ())
                t_lgt = snap("t")
                op("vector", lambda e, t=t, pb=pb: e.tensor_copy(
                    lg[:, t, :], pb[:, 0:9]), wait=[("t", t_lgt)],
                   inc=[("v", 1)])
                brel(bank, "v")
            v_lg = snap("v")
            tap_dump("logits", [("v", v_lg)],
                     [(lg[:].rearrange("p a b -> p (a b)"), 144, False)], SQ)

            op("scalar", lambda e: e.activation(
                e9[:].rearrange("p a b -> p (a b)"),
                lg[:].rearrange("p a b -> p (a b)"), AF.Exp),
                wait=[("v", v_lg)], inc=[("sc", 1)])
            sc_e9 = snap("sc")

            l8 = lg[:, :, 0:8]
            E8 = e9[:, :, 0:8]
            rw, npv, mk, ind, ta, tbuf, mask0 = (
                rbufs["rw"], rbufs["npv"], rbufs["mk"], rbufs["ind"],
                rbufs["ta"], rbufs["tb"], rbufs["mask0"])
            R = r16

            def bc8(x16):
                return x16.broadcast_to([128, NT, 8])

            def vop(fn, **kw):
                op("vector", fn, dep=True, **kw)

            v_mix0 = snap("v")
            op("vector", lambda e: e.memset(rw[:], 0.0), wait=[("sc", sc_e9)])
            op("vector", lambda e: e.memset(mask0[:], 0.0))
            op("vector", lambda e: e.memset(npv[:], 1.0))
            op("vector", lambda e: e.reduce_sum(R["z8"][:], E8, axis=AX.X))
            vop(lambda e: e.tensor_scalar(
                R["z8"][:], R["z8"][:], 0.75, None, ALU.mult))
            op("vector", lambda e: e.memset(R["cs"][:], 0.0))
            op("vector", lambda e: e.tensor_scalar(
                ta[:], l8, -1.0, None, ALU.mult))
            vop(lambda e: e.tensor_tensor(ta[:], ta[:], l8, ALU.max))
            for r in range(4):
                vop(lambda e: e.tensor_scalar(
                    mk[:], npv[:], BIG, None, ALU.mult))
                vop(lambda e: e.tensor_scalar(
                    mk[:], mk[:], -BIG, None, ALU.add))
                vop(lambda e: e.tensor_tensor(mk[:], mk[:], l8, ALU.add))
                vop(lambda e: e.reduce_max(R["thr"][:], mk[:], axis=AX.X))
                vop(lambda e: e.tensor_tensor(
                    ind[:], mk[:], bc8(R["thr"]), ALU.is_equal))
                vop(lambda e: e.tensor_copy(tbuf[:], ind[:]))
                vop(lambda e: e.tensor_copy(mk[:], ind[:]))
                vop(lambda e: e.tensor_add(
                    mk[:, :, 1:8], tbuf[:, :, 1:8], tbuf[:, :, 0:7]))
                vop(lambda e: e.tensor_copy(tbuf[:], mk[:]))
                vop(lambda e: e.tensor_add(
                    tbuf[:, :, 2:8], mk[:, :, 2:8], mk[:, :, 0:6]))
                vop(lambda e: e.tensor_copy(mk[:], tbuf[:]))
                vop(lambda e: e.tensor_add(
                    mk[:, :, 4:8], tbuf[:, :, 4:8], tbuf[:, :, 0:4]))
                vop(lambda e: e.tensor_tensor(
                    tbuf[:], mk[:], ind[:], ALU.subtract))
                vop(lambda e: e.tensor_scalar(
                    tbuf[:], tbuf[:], 0.0, None, ALU.is_equal))
                vop(lambda e: e.tensor_tensor(ind[:], ind[:], tbuf[:], ALU.mult))
                vop(lambda e: e.tensor_tensor(tbuf[:], ind[:], E8, ALU.mult))
                vop(lambda e: e.reduce_sum(R["ethr"][:], tbuf[:], axis=AX.X))
                vop(lambda e: e.tensor_scalar(
                    R["athr"][:], R["thr"][:], -1.0, None, ALU.mult))
                vop(lambda e: e.tensor_tensor(
                    R["athr"][:], R["athr"][:], R["thr"][:], ALU.max))
                vop(lambda e: e.tensor_tensor(
                    tbuf[:], ta[:], bc8(R["athr"]), ALU.max))
                vop(lambda e: e.tensor_scalar(
                    tbuf[:], tbuf[:], 0.2, None, ALU.mult))
                vop(lambda e: e.tensor_tensor(
                    mk[:], bc8(R["thr"]), l8, ALU.subtract))
                vop(lambda e: e.tensor_tensor(mk[:], mk[:], tbuf[:], ALU.is_le))
                vop(lambda e: e.tensor_tensor(mk[:], mk[:], npv[:], ALU.mult))
                vop(lambda e: e.tensor_tensor(tbuf[:], mk[:], E8, ALU.mult))
                vop(lambda e: e.reduce_sum(R["tmp1"][:], tbuf[:], axis=AX.X))
                vop(lambda e: e.reciprocal(R["tmp2"][:], R["tmp1"][:]))
                vop(lambda e: e.tensor_tensor(
                    R["mult"][:], R["ethr"][:], R["tmp2"][:], ALU.mult))
                if r == 0:
                    op("vector", lambda e: e.memset(R["valid"][:], 1.0))
                else:
                    vop(lambda e: e.tensor_tensor(
                        R["valid"][:], R["cs"][:], R["z8"][:], ALU.is_lt))
                vop(lambda e: e.tensor_tensor(
                    R["cs"][:], R["cs"][:], R["ethr"][:], ALU.add))
                vop(lambda e: e.tensor_tensor(
                    R["tmp1"][:], R["mult"][:], R["valid"][:], ALU.mult))
                vop(lambda e: e.tensor_tensor(
                    tbuf[:], ind[:], bc8(R["tmp1"]), ALU.mult))
                vop(lambda e: e.tensor_tensor(rw[:], rw[:], tbuf[:], ALU.add))
                vop(lambda e: e.tensor_tensor(
                    tbuf[:], ind[:], bc8(R["valid"]), ALU.mult))
                vop(lambda e: e.tensor_tensor(
                    mask0[:], mask0[:], tbuf[:], ALU.add))
                vop(lambda e: e.tensor_tensor(
                    npv[:], npv[:], ind[:], ALU.subtract))
            vop(lambda e: e.reduce_sum(R["tmp1"][:], rw[:], axis=AX.X))
            vop(lambda e: e.reciprocal(R["tmp2"][:], R["tmp1"][:]))
            vop(lambda e: e.tensor_tensor(
                rw[:], rw[:], bc8(R["tmp2"]), ALU.mult))
            vop(lambda e: e.tensor_copy(mask_bf[:], mask0[:]),
                inc=[("v", 1)])
            v_mix = snap("v")
            step = max(1, (v_mix - v_mix0) // 10)
            for k in range(10):
                warm(wait=(bwait("pm") if k == 0 else []) +
                     [("v", min(v_mix0 + k * step, v_mix))])

            w = list(bwait("pm")) + [("v", v_mix)]
            for t in range(NT):
                op("tensor", lambda e, t=t: e.matmul(
                    pm[0:1, t * 8:(t + 1) * 8], c_ones[:, 0:1],
                    mask_bf[:, t, :], start=True, stop=True),
                    wait=w if t == 0 else (),
                    inc=[("t", 1)] if t == NT - 1 else ())
            t_tot = snap("t")
            op("vector", lambda e: e.tensor_copy(
                cnt_row[0:1, :], pm[0:1, 0:128]),
               wait=[("t", t_tot)], inc=[("v", 1)])
            brel("pm", "v")
            cr = cnt_row[0:1, :].rearrange("p (t x) -> p t x", t=NT)
            op("vector", lambda e: e.reduce_sum(
                off_row[0:1, 0:8], cr.rearrange("p t x -> p x t"), axis=AX.X),
               inc=[("v", 1)], dep=True)
            v_cnts = snap("v")
            ag_state = [0]

            def emit_ag():
                op("gpsimd", lambda e, SQ=SQ: e.dma_start(
                    out=ag_in[SQ].ap(), in_=off_row[0:1, 0:8]),
                    wait=[("v", v_cnts)], inc=[("g", 16)])
                g0 = snap("g")
                op("gpsimd", lambda e, SQ=SQ: e.collective_compute(
                    "AllGather", ALU.bypass,
                    replica_groups=[list(range(NCORE))],
                    ins=[ag_in[SQ].ap()], outs=[ag_out[SQ].ap()]),
                    wait=[("g", g0)], inc=[("g", 1)])
                g1 = snap("g")
                op("gpsimd", lambda e, SQ=SQ: e.dma_start(
                    out=agsb[SQ][:], in_=ag_out[SQ].ap()),
                    wait=[("g", g1)], inc=[("g", 16)])
                ag_state[0] = snap("g")

            def emit_post_ag():
                w = list(bwait("pe")) + [("g", ag_state[0]),
                                     ("xld", consts_ld)]
                if SQ == 0:
                    op("tensor", lambda e: e.matmul(
                        pe[0:1, 0:8], meta_sb[:, 0:1], agsb[0][:],
                        start=True, stop=True), wait=w, inc=[("t", 1)])
                else:
                    op("tensor", lambda e: e.matmul(
                        pe[0:1, 0:8], meta_sb[:, 1:2], agsb[0][:],
                        start=True, stop=False), wait=w)
                    op("tensor", lambda e: e.matmul(
                        pe[0:1, 0:8], meta_sb[:, 0:1], agsb[1][:],
                        start=False, stop=True), inc=[("t", 1)])
                t_off = snap("t")
                op("vector", lambda e: e.tensor_copy(
                    off_row[0:1, 8:16], pe[0:1, 0:8]),
                   wait=[("t", t_off)], inc=[("v", 1)])
                brel("pe", "v")
                op("vector", lambda e: e.tensor_copy(
                    off_row[0:1, 16:24], off_row[0:1, 8:16]), dep=True)
                for t in range(1, NT):
                    op("vector", lambda e, t=t: e.tensor_tensor(
                        off_row[0:1, 16 + t * 8:24 + t * 8],
                        off_row[0:1, 8 + t * 8:16 + t * 8],
                        cnt_row[0:1, (t - 1) * 8:t * 8], ALU.add), dep=True)
                op("vector", lambda e: e.engine_nop(), inc=[("v", 1)], dep=True)
                v_offs = snap("v")
                w = list(bwait("pe")) + [("v", v_offs)]
                op("tensor", lambda e: e.matmul(
                    pe[:, 0:128], c_onesf[0:1, :], off_row[0:1, 16:144],
                    start=True, stop=True), wait=w, inc=[("t", 1)])
                t_obc = snap("t")
                op("vector", lambda e: e.tensor_copy(off_bc[:], pe[:, 0:128]),
                   wait=[("t", t_obc)], inc=[("v", 1)])
                brel("pe", "v")
                v_obc = snap("v")

                for t in range(NT):
                    bank = "s2" if t % 2 == 0 else "s3"
                    pb = ps4q(2) if t % 2 == 0 else ps4q(3)
                    w = list(bwait(bank))
                    if t == 0:
                        w.append(("v", v_obc))
                    op("tensor", lambda e, t=t, pb=pb: e.matmul(
                        pb[:, 0:8], c_tri[:], mask_bf[:, t, :],
                        start=True, stop=True), wait=w, inc=[("t", 1)])
                    t_cum = snap("t")
                    op("vector", lambda e, t=t, pb=pb: e.scalar_tensor_tensor(
                        mk[:, t, :], pb[:, 0:8], 1.0,
                        off_bc[:, t * 8:(t + 1) * 8], ALU.mult, ALU.add),
                        wait=[("t", t_cum)], dep=(t == 0))
                    op("vector", lambda e, t=t: e.scalar_tensor_tensor(
                        mask_sb[:, t, :], mk[:, t, :], CAP, mask0[:, t, :],
                        ALU.is_le, ALU.mult), inc=[("v", 1)], dep=True)
                    brel(bank, "v")
                v_mask = snap("v")

                op("vector", lambda e: e.tensor_tensor(
                    ta[:], e9[:, :, 0:8], mask_sb[:], ALU.mult),
                   wait=[("v", v_mask)], dep=True)
                vop(lambda e: e.reduce_sum(R["tmp1"][:], ta[:], axis=AX.X))
                vop(lambda e: e.tensor_tensor(
                    R["tmp2"][:], R["tmp1"][:], e9[:, :, 8], ALU.add))
                vop(lambda e: e.reciprocal(R["sts"][:], R["tmp2"][:]))
                vop(lambda e: e.tensor_tensor(
                    R["tmp1"][:], R["tmp1"][:], R["sts"][:], ALU.mult))
                vop(lambda e: e.tensor_tensor(
                    tbuf[:], rw[:], mask_sb[:], ALU.mult))
                vop(lambda e: e.tensor_tensor(
                    w9[:, :, 0:8], tbuf[:], bc8(R["tmp1"]), ALU.mult))
                vop(lambda e: e.tensor_tensor(
                    w9[:, :, 8], e9[:, :, 8], R["sts"][:], ALU.mult),
                    inc=[("v", 1)])
                v_w9 = snap("v")

                for t in range(NT):
                    half = t % 2
                    w = list(bwait("t%d" % half))
                    if t == 0:
                        w.append(("v", v_w9))
                    op("tensor", lambda e, t=t, half=half: e.transpose(
                        pth(half)[0:9, 0:128], w9[:, t, :], c_id[:]),
                        wait=w, inc=[("t", 1)])
                    t_w = snap("t")
                    op("vector", lambda e, t=t, half=half: e.tensor_copy(
                        w8[0:9, t * 128:(t + 1) * 128], pth(half)[0:9, 0:128]),
                        wait=[("t", t_w)], inc=[("v", 1)])
                    brel("t%d" % half, "v")
                v_w8 = snap("v")
                tap_dump("route", [("v", v_w8)],
                         [(w9[:].rearrange("p a b -> p (a b)"), 144, True),
                          (mask_sb[:].rearrange("p a b -> p (a b)"), 128, False),
                          (rw[:].rearrange("p a b -> p (a b)"), 128, False),
                          (mask0[:].rearrange("p a b -> p (a b)"), 128, False),
                          (off_bc[:, :], 128, False),
                          (e9[:].rearrange("p a b -> p (a b)"), 144, False)], SQ)

                return v_w8

            with contextlib.ExitStack() as ffn:
                A = ffn.enter_context
                uw = [{"g": A(nc.sbuf_tensor(f"uwg{SQ}_{i}",
                                             [128, NKC, 512], BF16)),
                       "u": A(nc.sbuf_tensor(f"uwu{SQ}_{i}",
                                             [128, NKC, 512], BF16))}
                      for i in range(2)]
                wd = [uw[i]["g"][:].rearrange("p a c -> p (a c)").rearrange(
                    "p (x y) -> p x y", x=4) for i in range(2)]
                ab = [A(nc.sbuf_tensor(f"a{SQ}_{u}", [128, 4, 512], BF16))
                      for u in range(NU)]
                sg = A(nc.sbuf_tensor(f"sg{SQ}", [128, 2, 512], BF16))
                wbc = A(nc.sbuf_tensor(f"wbc{SQ}", [128, 9, 512], BF16))

                guw_ld = [0, 0]
                guw_free = [("v", v_xf2), ("v", v_xf2)]
                wd_ld = [0, 0]
                wd_free = [("v", v_xf2), ("v", v_xf2)]
                sg_free = [("v", 0), ("v", 0)]
                a_guard = ("t", 0)
                wbc_guard = ("v", 0)

                def load_gu(uidx):
                    kind, z = UNITS[uidx]
                    i = uidx % 2
                    w = [guw_free[i]]
                    if guw_wb[i]:
                        w.append(("wld", guw_wb[i]))
                    if gu_cached[uidx]:
                        w.append(("wld", gu_cached[uidx]))
                        op("sync", lambda e, u=uidx, i=i: e.dma_start(
                            out=uw[i]["g"][:], in_=gu_cache.ap()[u, 0]),
                            wait=w, inc=[("wld", 16)])
                        op("sync", lambda e, u=uidx, i=i: e.dma_start(
                            out=uw[i]["u"][:], in_=gu_cache.ap()[u, 1]),
                            inc=[("wld", 16)])
                    else:
                        if kind == "d":
                            srcs = (wg_v[z], wu_v[z])
                        else:
                            srcs = (wsg_v[:, :, z * 512:(z + 1) * 512],
                                    wsu_v[:, :, z * 512:(z + 1) * 512])
                        op("gpsimd", lambda e, s=srcs[0], i=i: e.dma_start(
                            out=uw[i]["g"][:], in_=s),
                            wait=w, inc=[("wld", 16)])
                        ld_g = snap("wld")
                        op("gpsimd", lambda e, s=srcs[1], i=i: e.dma_start(
                            out=uw[i]["u"][:], in_=s),
                            inc=[("wld", 16)])
                        ld_u = snap("wld")
                        op("gpsimd", lambda e, u=uidx, i=i: e.dma_start(
                            out=gu_cache.ap()[u, 0], in_=uw[i]["g"][:]),
                            wait=[("wld", ld_g)], inc=[("wld", 16)])
                        op("gpsimd", lambda e, u=uidx, i=i: e.dma_start(
                            out=gu_cache.ap()[u, 1], in_=uw[i]["u"][:]),
                            wait=[("wld", ld_u)], inc=[("wld", 16)])
                        gu_cached[uidx] = snap("wld")
                        guw_wb[i] = snap("wld")
                    guw_ld[i] = snap("wld")

                def load_wd(uidx):
                    kind, z = UNITS[uidx]
                    i = uidx % 2
                    w = [wd_free[i]]
                    if wd_wb[i]:
                        w.append(("wld", wd_wb[i]))
                    if wd_cached[uidx]:
                        w.append(("wld", wd_cached[uidx]))
                        op("sync", lambda e, u=uidx, i=i: e.dma_start(
                            out=wd[i], in_=wd_cache.ap()[u]),
                            wait=w, inc=[("wld", 16)])
                    else:
                        src = (wd_v[z] if kind == "d"
                               else wsd_v[:, z * 4:(z + 1) * 4, :])
                        op("gpsimd", lambda e, s=src, i=i: e.dma_start(
                            out=wd[i], in_=s),
                            wait=w, inc=[("wld", 16)])
                        ld_d = snap("wld")
                        op("gpsimd", lambda e, u=uidx, i=i: e.dma_start(
                            out=wd_cache.ap()[u], in_=wd[i]),
                            wait=[("wld", ld_d)], inc=[("wld", 16)])
                        wd_cached[uidx] = snap("wld")
                        wd_wb[i] = snap("wld")
                    wd_ld[i] = snap("wld")

                GUB = [("s0", ps4q(0)), ("s1", ps4q(1)),
                       ("pe", pe[:, 0:512]), ("pm", pm[:, 0:512])]
                DBB = {(0, 0): ("s0", ps4q(0)), (0, 1): ("s1", ps4q(1)),
                       (1, 0): ("s2", ps4q(2)), (1, 1): ("s3", ps4q(3))}

                v_w8c = [0]

                def emit_wbc(tq0):
                    for wrow in range(9):
                        bank = "s2" if wrow % 2 == 0 else "s3"
                        pb = ps4q(2) if wrow % 2 == 0 else ps4q(3)
                        w = list(bwait(bank))
                        if wrow == 0:
                            w.append(("v", v_w8c[0]))
                        op("tensor", lambda e, wrow=wrow, tq0=tq0, pb=pb:
                           e.matmul(
                               pb[:, 0:512],
                               c_sel[:, wrow * 128:(wrow + 1) * 128],
                               w8[0:9, tq0:tq0 + 512],
                               start=True, stop=True), wait=w, inc=[("t", 1)])
                        t_wbc = snap("t")
                        op("scalar", lambda e, wrow=wrow, pb=pb: e.copy(
                            wbc[:, wrow, :], pb[:, 0:512]),
                           wait=[("t", t_wbc), wbc_guard], inc=[("sc", 1)])
                        brel(bank, "sc")
                    return snap("sc")

                for qtr in range(4):
                    tq0 = qtr * 512
                    if qtr > 0:
                        sc_wbc = emit_wbc(tq0)

                    load_gu(0)
                    for uidx in range(NU):
                        i = uidx % 2
                        if uidx + 1 < NU:
                            load_gu(uidx + 1)
                        wrow = min(uidx, 8)
                        for ic in range(4):
                            par = (uidx * 4 + ic) % 2
                            gb, gpb = GUB[par * 2]
                            ub, upb = GUB[par * 2 + 1]
                            w = list(bwait(gb)) + [("wld", guw_ld[i])]
                            if uidx == 0 and ic == 0:
                                w.append(a_guard)
                            for kc in range(NKC):
                                op("tensor", lambda e, i=i, ic=ic, kc=kc,
                                   gpb=gpb, tq0=tq0: e.matmul(
                                       gpb, uw[i]["g"][:, kc,
                                                       ic * 128:(ic + 1) * 128],
                                       xf[:, kc, tq0:tq0 + 512],
                                       start=(kc == 0), stop=(kc == NKC - 1)),
                                    wait=w if kc == 0 else (),
                                    inc=[("t", 1)] if kc == NKC - 1 else ())
                            t_g = snap("t")
                            w2 = list(bwait(ub))
                            for kc in range(NKC):
                                op("tensor", lambda e, i=i, ic=ic, kc=kc,
                                   upb=upb, tq0=tq0: e.matmul(
                                       upb, uw[i]["u"][:, kc,
                                                       ic * 128:(ic + 1) * 128],
                                       xf[:, kc, tq0:tq0 + 512],
                                       start=(kc == 0), stop=(kc == NKC - 1)),
                                    wait=w2 if kc == 0 else (),
                                    inc=[("t", 1)] if kc == NKC - 1 else ())
                            t_u = snap("t")
                            op("scalar", lambda e, par=par, gpb=gpb:
                               e.activation(sg[:, par, :], gpb, AF.Silu),
                               wait=[("t", t_g), sg_free[par]],
                               inc=[("sc", 1)])
                            sc_sg = snap("sc")
                            brel(gb, "sc")
                            wv_ = [("t", t_u), ("sc", sc_sg)]
                            op("vector", lambda e, uidx=uidx, ic=ic, par=par,
                               upb=upb: e.tensor_tensor(
                                   ab[uidx][:, ic, :], upb, sg[:, par, :],
                                   ALU.mult), wait=wv_, inc=[("v", 1)])
                            brel(ub, "v")
                            sg_free[par] = ("v", snap("v"))
                        guw_free[i] = ("t", snap("t"))

                    if qtr == 0:
                        emit_ag()
                        v_w8c[0] = emit_post_ag()
                        sc_wbc = emit_wbc(tq0)
                    tt2_done = []
                    for uidx in range(NU):
                        wrow = min(uidx, 8)
                        for ic in range(4):
                            op("vector", lambda e, uidx=uidx, ic=ic,
                               wrow=wrow: e.tensor_tensor(
                                   ab[uidx][:, ic, :], ab[uidx][:, ic, :],
                                   wbc[:, wrow, :], ALU.mult),
                               wait=[("sc", sc_wbc)] if (
                                   uidx == 0 and ic == 0) else (),
                               inc=[("v", 1)])
                        tt2_done.append(snap("v"))
                    v_aall = snap("v")
                    wbc_guard = ("v", v_aall)

                    wd_free[0] = guw_free[0]
                    wd_free[1] = guw_free[1]
                    for eighth in range(2):
                        load_wd(0)
                        for uidx in range(NU):
                            i = uidx % 2
                            if uidx + 1 < NU:
                                load_wd(uidx + 1)
                            nmm = 0
                            for tile in range(2):
                                for half in range(2):
                                    bank, pb = DBB[(tile, half)]
                                    for ic in range(4):
                                        nmm += 1
                                        wd_ = []
                                        if uidx == 0 and ic == 0:
                                            wd_ = list(bwait(bank))
                                        if nmm == 1:
                                            wd_.append(("v", tt2_done[uidx]))
                                            wd_.append(("wld", wd_ld[i]))
                                        op("tensor",
                                           lambda e, uidx=uidx, i=i, ic=ic,
                                           tile=tile, half=half, pb=pb,
                                           eighth=eighth,
                                           s=(uidx == 0 and ic == 0),
                                           p=(uidx == NU - 1 and ic == 3):
                                           e.matmul(
                                               pb,
                                               ab[uidx][:, ic,
                                                        eighth * 256 +
                                                        tile * 128:
                                                        eighth * 256 +
                                                        (tile + 1) * 128],
                                               wd[i][:, ic,
                                                     half * 512:
                                                     (half + 1) * 512],
                                               start=s, stop=p),
                                            wait=wd_,
                                            inc=[("t", 1)] if nmm == 16
                                            else ())
                            t_dn = snap("t")
                            wd_free[i] = ("t", t_dn)
                        t_all = snap("t")
                        for tile in range(2):
                            Tt = qtr * 4 + eighth * 2 + tile
                            for half in range(2):
                                bank, pb = DBB[(tile, half)]
                                op("vector", lambda e, Tt=Tt, half=half,
                                   pb=pb: e.tensor_tensor(
                                       x_sb[:, Tt,
                                            half * 512:(half + 1) * 512],
                                       pb,
                                       x_sb[:, Tt,
                                            half * 512:(half + 1) * 512],
                                       ALU.add),
                                   wait=[("t", t_all)] if (tile == 0 and
                                                           half == 0)
                                   else (), inc=[("v", 1)])
                                brel(bank, "v")
                        v_q = snap("v")
                        q8 = qtr * 2 + eighth
                        op("sync", lambda e, SQ=SQ, q8=q8: e.dma_start(
                            out=out_t[SQ, q8 * 2:(q8 + 1) * 2].rearrange(
                                "t p d -> p t d"),
                            in_=x_sb[:, q8 * 2:(q8 + 1) * 2, :]),
                            wait=[("v", v_q)], inc=[("st", 16)])
                        if SQ == 0:
                            st_now = snap("st")
                            op("sync", lambda e, q8=q8: e.dma_start(
                                out=x_sb[:, q8 * 2:(q8 + 1) * 2, :],
                                in_=xc_t[1, q8 * 2:(q8 + 1) * 2].rearrange(
                                    "t p d -> p t d")),
                                wait=[("st", st_now)], inc=[("xld", 16)])
                            x_ld8.append(snap("xld"))
                    a_guard = ("t", snap("t"))
                    guw_free[0] = a_guard
                    guw_free[1] = a_guard
            store_done[SQ] = snap("st")
            if SQ == 0:
                x_ld[1] = snap("xld")

        op("sync", lambda e: e.nop(), wait=[("st", store_done[1])])

        def play_all(e, eng):
            for rec in ops[eng]:
                for s, v in rec["wait"]:
                    if v > 0:
                        e.wait_ge(sems[s], v)
                ins = rec["fn"](e)
                for s, n in rec["inc"]:
                    ins.then_inc(sems[s], n)

        with nc.Block() as blk:
            @blk.sync
            def _(e):
                play_all(e, "sync")

            @blk.gpsimd
            def _(e):
                play_all(e, "gpsimd")

            @blk.tensor
            def _(e):
                play_all(e, "tensor")

            @blk.vector
            def _(e):
                play_all(e, "vector")

            @blk.scalar
            def _(e):
                play_all(e, "scalar")

    return nc


_NC_CACHE = None


def kernel(L0_attn_x, L0_attn_w_Q, L0_attn_w_K, L0_attn_w_V, L0_ffn_gate_w,
           L0_ffn_dyn_gate_w, L0_ffn_dyn_up_w, L0_ffn_dyn_down_w,
           L0_ffn_sh_gate_w, L0_ffn_sh_up_w, L0_ffn_sh_down_w):
    x = np.ascontiguousarray(np.asarray(L0_attn_x, dtype=np.float32))
    wq = np.ascontiguousarray(np.asarray(L0_attn_w_Q, dtype=np.float32))
    wk = np.ascontiguousarray(np.asarray(L0_attn_w_K, dtype=np.float32))
    wv = np.ascontiguousarray(np.asarray(L0_attn_w_V, dtype=np.float32))
    gwt = np.ascontiguousarray(np.asarray(L0_ffn_gate_w, dtype=np.float32).T)
    wg_t = np.ascontiguousarray(
        np.asarray(L0_ffn_dyn_gate_w, dtype=np.float32).transpose(0, 2, 1))
    wu_t = np.ascontiguousarray(
        np.asarray(L0_ffn_dyn_up_w, dtype=np.float32).transpose(0, 2, 1))
    wd_t = np.ascontiguousarray(
        np.asarray(L0_ffn_dyn_down_w, dtype=np.float32).transpose(0, 2, 1))
    wsg_t = np.ascontiguousarray(
        np.asarray(L0_ffn_sh_gate_w, dtype=np.float32)[0].T)
    wsu_t = np.ascontiguousarray(
        np.asarray(L0_ffn_sh_up_w, dtype=np.float32)[0].T)
    wsd_t = np.ascontiguousarray(
        np.asarray(L0_ffn_sh_down_w, dtype=np.float32)[0].T)

    global _NC_CACHE
    if _NC_CACHE is None or _NC_CACHE[0] != TAP:
        _NC_CACHE = (TAP, build_nc(tap=TAP))
    nc = _NC_CACHE[1]

    in_maps = []
    for c in range(NCORE):
        m = np.zeros((8, 2), dtype=np.float32)
        m[0:c, 0] = 1.0
        m[:, 1] = 1.0
        in_maps.append({
            "xc": np.ascontiguousarray(
                np.concatenate([x[c], x[c + 8]], axis=0)),
            "wq": wq, "wk": wk, "wv": wv, "gwt": gwt,
            "wg_t": wg_t, "wu_t": wu_t, "wd_t": wd_t,
            "wsg_t": wsg_t, "wsu_t": wsu_t, "wsd_t": wsd_t,
            "meta": m,
        })

    res = run_bass_kernel_spmd(nc, in_maps, core_ids=list(range(NCORE)))
    global _LAST_RES
    _LAST_RES = res
    out = np.empty((16, T, D), dtype=np.float32)
    for c in range(NCORE):
        oc = res.results[c]["outc"]
        out[c] = oc[0:T]
        out[c + 8] = oc[T:2 * T]
    return out


_LAST_RES = None
